# version 09
# baseline (speedup 1.0000x reference)
import numpy as np
import contextlib

import numpy as np
import concourse.bass as bass
import concourse.bacc as bacc
import concourse.mybir as mybir

F32 = mybir.dt.float32
BF16 = mybir.dt.bfloat16
FP8 = mybir.dt.float8e4
I32 = mybir.dt.int32
AF = mybir.ActivationFunctionType
ALU = mybir.AluOpType
AX = mybir.AxisListType
DR = mybir.MatmulPerfMode.DoubleRow

NCORE = 8
S = 8192
DM = 2048
NH = 16
DH = 128
DI = 5632
TPC = S // NCORE
HPC = NH // NCORE

class Sem:
    def __init__(self, handle):
        self.h = handle
        self.n = 0
    def inc(self, k=1):
        self.n += k
        return k

def mk_sems(nc, names):
    import contextlib
    stack = contextlib.ExitStack()
    out = {}
    for n in names:
        out[n] = Sem(stack.enter_context(nc.semaphore(n)))
    return stack, out

class Counter:
    def __init__(self):
        self.n = 0
    def reserve(self, k=1):
        self.n += k
        return self.n


"""Fused single-launch prb depth-1 megakernel.

Per core: token-sharded LN+QKV prep (all heads, own 1024 tokens) ->
on-chip all-to-all (remote DMA) of transposed q/k, v, rowmax rows ->
head-sharded causal attention (2 heads, all tokens) -> y all-to-all ->
token-sharded FFN (fp8) + out-projection + residual.
"""

PNC_MAP = (4, 5, 6, 7, 2, 3, 0, 1)
HOST_RID = (0, 8, 9, 1, 2, 10, 11, 3, 6, 14, 15, 7, 4, 12, 13, 5)
RMASK = 0xf0f0

NTC = TPC // 128
NF = DM // 128
NPIECE = (3 * DM) // 512
NQB = S // 512
NIP = DI // 512
NFP = DM // 256
NP2 = DI // 256
NHP = DM // 512
NDC = DM // 128
Q2C = 4.0 / (448.0 * 448.0)
RWG_N, RWU_N, RWD_N = DI // 8, DI // 8, DM // 8
RWROW = RWG_N + RWU_N + RWD_N

def build_fused():
    nc = bacc.Bacc("TRN2")
    from concourse.library_config import proxy as proxy_lib

    x = nc.dram_tensor("x", [TPC, DM], F32, kind="ExternalInput").ap()
    w_qkv = nc.dram_tensor("w_qkv", [DM, 3 * DM], F32, kind="ExternalInput").ap()
    cos_t = nc.dram_tensor("cos_t", [TPC, 64], F32, kind="ExternalInput").ap()
    sin_t = nc.dram_tensor("sin_t", [TPC, 64], F32, kind="ExternalInput").ap()
    gain_b = nc.dram_tensor("gain_b", [128, NH], F32, kind="ExternalInput").ap()
    ident = nc.dram_tensor("ident", [128, 128], BF16, kind="ExternalInput").ap()
    id8 = nc.dram_tensor("id8", [128, 128], FP8, kind="ExternalInput").ap()
    wg_sh = nc.dram_tensor("wg_sh", [RWG_N, DM], F32, kind="ExternalInput").ap()
    wu_sh = nc.dram_tensor("wu_sh", [RWU_N, DM], F32, kind="ExternalInput").ap()
    wd_sh = nc.dram_tensor("wd_sh", [RWD_N, DI], F32, kind="ExternalInput").ap()
    m01 = nc.dram_tensor("m01", [128, 1024], BF16, kind="ExternalInput").ap()
    m23 = nc.dram_tensor("m23", [128, 1024], BF16, kind="ExternalInput").ap()
    w_out = nc.dram_tensor("w_out", [DM, DM], F32, kind="ExternalInput").ap()
    wgT = nc.dram_tensor("wgT", [DM, DI], F32, kind="ExternalInput").ap()
    wuT = nc.dram_tensor("wuT", [DM, DI], F32, kind="ExternalInput").ap()
    wd2T = nc.dram_tensor("wd2T", [DI, DM], F32, kind="ExternalInput").ap()
    out_o = nc.dram_tensor("out_o", [TPC, DM], F32, kind="ExternalOutput").ap()
    xn_d = nc.dram_tensor("xn_d", [TPC, DM], F32, kind="Internal").ap()
    t_spill = nc.dram_tensor("t_spill", [TPC, DI], F32, kind="Internal").ap()
    rw_loc = nc.dram_tensor("rw_loc", [RWROW], F32, kind="Internal").ap()
    stg_d = nc.dram_tensor("stg_d", [12, 128, 4096], BF16, kind="Internal").ap()
    rw_full = nc.dram_tensor("rw_full", [NCORE * RWROW], F32, kind="Internal").ap()

    cL = {k: Counter() for k in ["v", "s", "t", "evT"]}
    L = []
    for tc in range(NTC):
        d = {"xc": cL["v"].reserve(), "var": cL["v"].reserve(), "sqrt": cL["s"].reserve(),
             "xnr": cL["v"].reserve(), "xnb": cL["v"].reserve(),
             "tr": [cL["t"].reserve() for _ in range(NF)],
             "trev": [cL["evT"].reserve() for _ in range(NF)]}
        L.append(d)
    NCONST = 2 + 2 * NTC

    c = {k: Counter() for k in ["mm", "ev", "sq", "red", "rsq", "app", "cstV", "cstS",
                                "tr2", "evQ", "stv"]}
    G = []
    for piece in range(NPIECE):
        for tc in range(NTC):
            g = {"piece": piece, "tc": tc, "seg": piece // 4}
            g["mm"] = c["mm"].reserve()
            g["ev"] = c["ev"].reserve()
            if g["seg"] < 2:
                g["sq"] = c["sq"].reserve()
                g["red"] = c["red"].reserve()
                g["rsq"] = c["rsq"].reserve()
                g["app"] = c["app"].reserve()
                g["cst_sem"], g["cst"] = "cstS", c["cstS"].reserve()
                g["tr2"] = [c["tr2"].reserve() for _ in range(4)]
                g["evQ"] = [c["evQ"].reserve() for _ in range(4)]
            else:
                g["cst_sem"], g["cst"] = "cstV", c["cstV"].reserve()
                g["stv"] = [c["stv"].reserve() for _ in range(2)]
            G.append(g)
    def piece_groups(piece):
        return [G[piece * NTC + tc] for tc in range(NTC)]

    cA = {k: Counter() for k in ["mm", "exp", "acc", "av", "dn", "rcp", "bc", "yn", "yst"]}
    HQ = []
    for h in range(HPC):
        for qb in range(NQB):
            grps = [{"mm": cA["mm"].reserve(), "exp": cA["exp"].reserve(),
                     "acc": cA["acc"].reserve(), "av": cA["av"].reserve()}
                    for _ in range(2 * (qb + 1))]
            d = {"h": h, "qb": qb, "grps": grps, "dn": cA["dn"].reserve(),
                 "rcp": cA["rcp"].reserve(), "bc": cA["bc"].reserve(),
                 "yn": cA["yn"].reserve(), "yst": cA["yst"].reserve()}
            HQ.append(d)

    cdw = {0: Counter(), 1: Counter()}
    cqw = Counter(); cmm3 = Counter(); cev3 = Counter()
    csl = Counter(); ctd = Counter(); crp = Counter()
    SCH = []
    for ip in range(NIP):
        nd = 0
        def _rw_piece_dmas(kind, start, length):
            base = {"g": 0, "u": DI // 8 * 0 + RWG_N, "d": RWG_N + RWU_N}[kind]
            per = {"g": RWG_N, "u": RWU_N, "d": RWD_N}[kind]
            out = []
            pos = start
            while pos < start + length:
                cidx = pos // per
                off_in = pos - cidx * per
                ln = min(per - off_in, start + length - pos)
                out.append((cidx * RWROW + base + off_in, ln))
                pos += ln
            return out
        nd = len(_rw_piece_dmas("g", ip * 512, 512)) + len(_rw_piece_dmas("u", ip * 512, 512))
        d = {"ip": ip, "rp": crp.reserve(16 * nd)}
        for w in ("g", "u"):
            dws, qws = [], []
            for fp in range(NFP):
                par = fp % 2
                dws.append((f"dw{par}", cdw[par].reserve(32)))
                qws.append(cqw.reserve())
            d["dw" + w] = dws
            d["qw" + w] = qws
            d["mm" + w] = [cmm3.reserve() for _ in range(NTC)]
        d["ev"] = [cev3.reserve() for _ in range(NTC)]
        d["sil"] = [csl.reserve() for _ in range(NTC)]
        d["td"] = [ctd.reserve() for _ in range(NTC)]
        SCH.append(d)
    cqwD = Counter(); cmmD = Counter(); cevD = Counter(); crpD = Counter()
    SD = []
    for hp in range(NHP):
        ndd = 1 if (hp * 512) // RWD_N == (hp * 512 + 511) // RWD_N else 2
        d = {"hp": hp, "dw": [], "qw": [], "rp": crpD.reserve(16 * ndd)}
        for ip2 in range(NP2):
            par = ip2 % 2
            d["dw"].append((f"dw{par}", cdw[par].reserve(32)))
            d["qw"].append(cqwD.reserve())
        d["mm"] = [cmmD.reserve() for _ in range(NTC)]
        d["ev"] = [cevD.reserve() for _ in range(NTC)]
        SD.append(d)
    cmmE = Counter(); cevE = Counter(); cwu = Counter()
    cdE = {0: Counter(), 1: Counter()}
    SE = []
    for op in range(NHP):
        d = {"op": op, "dw": [], "wu": []}
        for dc in range(NDC):
            par = dc % 2
            d["dw"].append((f"dwE{par}", cdE[par].reserve(16)))
            d["wu"].append(cwu.reserve() if dc < NDC - 1 else None)
        d["mm"] = [cmmE.reserve() for _ in range(NTC)]
        d["ev"] = [cevE.reserve() for _ in range(NTC)]
        SE.append(d)

    def rw_piece_dmas(kind, start, length):
        base = {"g": 0, "u": RWG_N, "d": RWG_N + RWU_N}[kind]
        per = {"g": RWG_N, "u": RWU_N, "d": RWD_N}[kind]
        out = []
        pos = start
        while pos < start + length:
            cidx = pos // per
            off_in = pos - cidx * per
            ln = min(per - off_in, start + length - pos)
            out.append((cidx * RWROW + base + off_in, ln))
            pos += ln
        return out

    st = contextlib.ExitStack()
    e = st.enter_context
    with st:
        rws_g = e(nc.sbuf_tensor("rws_g", [128, NCORE * 13], F32))
        rwrow = e(nc.sbuf_tensor("rwrow", [128, 13], F32))
        eps5 = e(nc.sbuf_tensor("eps5", [128, 1], F32))
        eps6 = e(nc.sbuf_tensor("eps6", [128, 1], F32))
        half_neg = e(nc.sbuf_tensor("half_neg", [128, 1], F32))
        id8b = e(nc.sbuf_tensor("id8b", [128, 128], FP8))
        idn = e(nc.sbuf_tensor("idn", [128, 128], BF16))
        wrm = e(nc.sbuf_tensor("wrm", [128, 16], F32))
        wrs = e(nc.sbuf_tensor("wrs", [128, 2], F32))
        warm_preps = [0]

        names = ["dx", "did", "dw", "dwq", "dout0", "dout1", "vL", "sL", "tT", "evT",
                 "mm", "ev", "ssq_s", "red", "rsq", "app", "cstV", "cstS",
                 "tr2", "evQ", "stv", "rx", "drow",
                 "ps1", "rqk", "rv", "rrw", "lself0", "lself1", "lsnd0", "lsnd1", "lrw", "xdone", "dbn0", "dbn1", "dstg0", "dstg1",
                 "din2", "mm2", "expd", "accd", "avd", "dnd", "rcpd", "bcd", "rnd", "yst",
                 "ps2", "ry", "lself2", "lsnd2",
                 "drp", "vA", "tA", "evA",
                 "dw0", "dw1", "qwB", "mmB", "evB", "silB", "tdB", "dsp0", "dsp1", "bcG",
                 "dtC", "vC", "tC", "evC",
                 "qwD", "mmD", "evD", "bcD",
                 "dwE0", "dwE1", "bufE", "mmE", "evE",
                 "dxF", "vF", "doF0", "doF1", "rwa", "rwm", "lwm"]
        stk, sm = mk_sems(nc, names)
        e(stk)

        stR1 = contextlib.ExitStack()
        xs = stR1.enter_context(nc.sbuf_tensor("xs", [128, DM], F32, side="right"))
        xn = [stR1.enter_context(nc.sbuf_tensor(f"xn{i}", [128, DM], F32, side="right")) for i in range(2)]
        xc = stR1.enter_context(nc.sbuf_tensor("xc", [128, DM], F32, side="right"))
        sq_t = stR1.enter_context(nc.sbuf_tensor("sq_t", [128, DM], F32, side="right"))
        st1 = contextlib.ExitStack()
        e1 = st1.enter_context
        stat = e1(nc.sbuf_tensor("stat", [128, 4], F32))
        rstd_all = e1(nc.sbuf_tensor("rstd_all", [128, NTC], F32))
        xnb = e1(nc.sbuf_tensor("xnb", [128, DM], BF16))
        xnT = e1(nc.sbuf_tensor("xnT", [128, NTC * DM], BF16))
        cosb = e1(nc.sbuf_tensor("cosb", [128, NTC * 64], F32))
        sinb = e1(nc.sbuf_tensor("sinb", [128, NTC * 64], F32))
        gainb = e1(nc.sbuf_tensor("gainb", [128, NH], F32))
        wq = [e1(nc.sbuf_tensor(f"wqb{i}", [128, NF * 512], BF16)) for i in range(2)]
        wk = [e1(nc.sbuf_tensor(f"wk{i}", [128, 512], F32)) for i in range(2)]
        ssq = [e1(nc.sbuf_tensor(f"ssq{i}", [128, 512], F32)) for i in range(2)]
        rn = [e1(nc.sbuf_tensor(f"rn{i}", [128, 4], F32)) for i in range(2)]
        rtmp = e1(nc.sbuf_tensor("rtmp", [128, 256], F32))
        stg = [e1(nc.sbuf_tensor(f"stg{i}", [128, 512], BF16)) for i in range(2)]
        qks = [e1(nc.sbuf_tensor(f"qks{i}", [128, 4096], BF16)) for i in range(2)]
        vst = [e1(nc.sbuf_tensor(f"vst{i}", [128, 4096], BF16)) for i in range(2)]

        ptr = [e1(nc.psum_tensor(f"ptr{i}", [128, 128], BF16)) for i in range(2)]
        ps = [e1(nc.psum_tensor(f"ps{i}", [128, 512], F32)) for i in range(6)]

        with nc.Block() as blk:
            @blk.vector
            def _(v):
                v.memset(eps5[:, :], 1e-5)
                v.memset(eps6[:, :], 1e-6)
                v.memset(half_neg[:, :], -0.5)
                v.memset(wrm[:, :], 0.0)
                v.memset(wrs[:, :], 0.0)

        with nc.Block() as blk:
            @blk.sync
            def _(sync):
                sync.dma_start(idn[:, :], ident[:, :]).then_inc(sm["dx"].h, sm["dx"].inc(16))
                sync.dma_start(id8b[:, :], id8[:, :]).then_inc(sm["did"].h, sm["did"].inc(16))
                sync.dma_start(gainb[:, :], gain_b[:, :]).then_inc(sm["dx"].h, sm["dx"].inc(16))
                for tc in range(NTC):
                    sync.dma_start(cosb[:, tc * 64:(tc + 1) * 64],
                                   cos_t[tc * 128:(tc + 1) * 128, :]).then_inc(sm["dx"].h, sm["dx"].inc(16))
                    sync.dma_start(sinb[:, tc * 64:(tc + 1) * 64],
                                   sin_t[tc * 128:(tc + 1) * 128, :]).then_inc(sm["dx"].h, sm["dx"].inc(16))
                for tc in range(NTC):
                    if tc >= 1:
                        sync.wait_ge(sm["vL"].h, L[tc - 1]["xc"])
                    sync.dma_start(xs[:, :], x[tc * 128:(tc + 1) * 128, :]).then_inc(sm["dx"].h, sm["dx"].inc(16))

            @blk.vector
            def _(v):
                v.wait_ge(sm["rwm"].h, 7 * 8)
                for tc in range(NTC):
                    v.wait_ge(sm["dx"].h, 16 * (NCONST + tc + 1))
                    if tc >= 2:
                        v.wait_ge(sm["dout0" if tc % 2 == 0 else "dout1"].h, 16 * ((tc - 2) // 2 + 1))
                    v.tensor_reduce(stat[:, 0:1], xs[:, :], AX.X, ALU.add, negate=True)
                    v.drain()
                    v.tensor_scalar(stat[:, 0:1], stat[:, 0:1], 1.0 / DM, None, ALU.mult)
                    v.drain()
                    v.tensor_scalar(xc[:, :], xs[:, :], stat[:, 0:1], None, ALU.add)
                    v.drain()
                    v.sem_inc(sm["vL"].h, 1)
                    v.tensor_tensor(sq_t[:, :], xc[:, :], xc[:, :], ALU.mult)
                    v.drain()
                    v.tensor_reduce(stat[:, 1:2], sq_t[:, :], AX.X, ALU.add)
                    v.drain()
                    v.tensor_scalar(stat[:, 1:2], stat[:, 1:2], 1.0 / DM, None, ALU.mult)
                    v.drain()
                    v.sem_inc(sm["vL"].h, 1)
                    v.wait_ge(sm["sL"].h, L[tc]["sqrt"])
                    v.reciprocal(rstd_all[:, tc:tc + 1], stat[:, 2:3])
                    v.drain()
                    v.tensor_scalar(xn[tc % 2][:, :], xc[:, :], rstd_all[:, tc:tc + 1], None, ALU.mult)
                    v.drain()
                    v.sem_inc(sm["vL"].h, 1)
                    v.tensor_copy(xnb[:, :], xn[tc % 2][:, :])
                    v.drain()
                    v.sem_inc(sm["vL"].h, 1)
                    for f in range(NF):
                        v.wait_ge(sm["tT"].h, L[tc]["tr"][f])
                        v.tensor_copy(xnT[:, tc * DM + f * 128: tc * DM + (f + 1) * 128],
                                      ptr[f % 2][:, :]).then_inc(sm["evT"].h, 1)

            @blk.scalar
            def _(s):
                for tc in range(NTC):
                    s.wait_ge(sm["vL"].h, L[tc]["var"])
                    s.activation(stat[:, 2:3], stat[:, 1:2], AF.Sqrt, bias=eps5[:, 0:1], scale=1.0).then_inc(sm["sL"].h, 1)

            @blk.tensor
            def _(t):
                t.wait_ge(sm["dx"].h, 16 * (NCONST + 1))
                for tc in range(NTC):
                    t.wait_ge(sm["vL"].h, L[tc]["xnb"])
                    for f in range(NF):
                        n = tc * NF + f
                        if n >= 2:
                            tcp, fp = divmod(n - 2, NF)
                            t.wait_ge(sm["evT"].h, L[tcp]["trev"][fp])
                        t.transpose(ptr[f % 2][:, :], xnb[:, f * 128:(f + 1) * 128], idn[:, :]).then_inc(sm["tT"].h, 1)

            @blk.gpsimd
            def _(g):
                g.load_library(proxy_lib)
                for arm in nc.switch_core_id([g], n=128):
                    hd, pnc = int(arm) // 8, int(arm) % 8
                    if pnc not in PNC_MAP:
                        continue
                    me = PNC_MAP.index(pnc)
                    rid = HOST_RID[hd]
                    kw = 0
                    for dj in range(NCORE):
                        if dj == me:
                            continue
                        g.remote_dma(wrm[:, me * 2:(me + 1) * 2], wrs[:, :], sm["rwm"].h, sm["lwm"].h,
                                     pid=PNC_MAP[dj], routing_id=rid,
                                     dma_engine_mask=RMASK).then_inc(sm["ps1"].h, 1)
                        kw += 1
                        g.wait_ge(sm["ps1"].h, kw)
                        g.trigger_dma(1)
                    warm_preps[0] = kw
                for q in range(2):
                    dsem = sm["dw"] if q % 2 == 0 else sm["dwq"]
                    for f in range(NF):
                        g.dma_start(wq[q][:, f * 512:(f + 1) * 512],
                                    w_qkv[f * 128:(f + 1) * 128, q * 512:(q + 1) * 512]).then_inc(dsem.h, dsem.inc(16))
                for tc in range(NTC):
                    g.wait_ge(sm["vL"].h, L[tc]["xnr"])
                    dsem = sm["dout0"] if tc % 2 == 0 else sm["dout1"]
                    g.dma_start(xn_d[tc * 128:(tc + 1) * 128, :], xn[tc % 2][:, :]).then_inc(dsem.h, dsem.inc(16))

        stR1.close()
        stR2 = contextlib.ExitStack()
        wsh = stR2.enter_context(nc.sbuf_tensor("wsh", [128, DI // 2], F32, side="right"))
        rwcol = stR2.enter_context(nc.sbuf_tensor("rwcol", [128, 16], F32, side="right"))
        rmcB = stR2.enter_context(nc.sbuf_tensor("rmcB", [128, 1], F32, side="right"))

        with nc.Block() as blk:
            @blk.gpsimd
            def _(g):
                for pg in range(12):
                    if pg + 2 <= NPIECE - 1:
                        q = pg + 2
                        g.wait_ge(sm["mm"].h, piece_groups(pg)[-1]["mm"])
                        dsem = sm["dw"] if q % 2 == 0 else sm["dwq"]
                        for f in range(NF):
                            g.dma_start(wq[q % 2][:, f * 512:(f + 1) * 512],
                                        w_qkv[f * 128:(f + 1) * 128, q * 512:(q + 1) * 512]).then_inc(dsem.h, dsem.inc(16))
                    grps = piece_groups(pg)
                    if pg < 8:
                        g.wait_ge(sm["evQ"].h, grps[-1]["evQ"][3])
                        srcb = qks[pg % 2]
                    else:
                        g.wait_ge(sm["stv"].h, grps[-1]["stv"][1])
                        srcb = vst[pg % 2]
                    psem = "dstg0" if pg % 2 == 0 else "dstg1"
                    g.dma_start(stg_d[pg, :, :], srcb[:, :]).then_inc(sm[psem].h, 16)

            @blk.tensor
            def _(t):
                for n, g in enumerate(G):
                    piece, tc = g["piece"], g["tc"]
                    if tc == 0:
                        dsem = sm["dw"] if piece % 2 == 0 else sm["dwq"]
                        t.wait_ge(dsem.h, 16 * NF * (piece // 2 + 1))
                    if n >= 6:
                        t.wait_ge(sm["ev"].h, G[n - 6]["ev"])
                    bank = n % 6
                    for f in range(NF):
                        m = t.matmul(ps[bank][:, :],
                                     xnT[:, tc * DM + f * 128:tc * DM + (f + 1) * 128],
                                     wq[piece % 2][:, f * 512:(f + 1) * 512],
                                     start=(f == 0), stop=(f == NF - 1))
                        if f == NF - 1:
                            m.then_inc(sm["mm"].h, 1)
                    if g["seg"] < 2:
                        t.wait_ge(sm[g["cst_sem"]].h, g["cst"])
                        for hs in range(4):
                            nq = (n * 4 + hs)
                            if g["tr2"][hs] > 2 * 4:
                                pass
                            t.wait_ge(sm["evQ"].h, max(0, g["evQ"][hs] - 2))
                            t.transpose(ptr[hs % 2][:, :], stg[n % 2][:, hs * 128:(hs + 1) * 128],
                                        idn[:, :]).then_inc(sm["tr2"].h, 1)

            @blk.vector
            def _(v):
                for n, g in enumerate(G):
                    tc, seg = g["tc"], g["seg"]
                    b2 = n % 2
                    bank = n % 6
                    v.wait_ge(sm["mm"].h, g["mm"])
                    if n >= 2:
                        gp = G[n - 2]
                        v.wait_ge(sm[gp["cst_sem"]].h, gp["cst"])
                        if gp["seg"] < 2:
                            v.wait_ge(sm["evQ"].h, gp["evQ"][3])
                    if seg == 2:
                        pg = g["piece"]
                        if tc == 0 and pg >= 10:
                            psem = "dstg0" if pg % 2 == 0 else "dstg1"
                            v.wait_ge(sm[psem].h, 16 * ((pg - 2) // 2 + 1))
                        for dj in range(2):
                            v.tensor_copy(vst[pg % 2][:, dj * 2048 + tc * 256: dj * 2048 + (tc + 1) * 256],
                                          ps[bank][:, dj * 256:(dj + 1) * 256])
                        v.drain()
                        v.sem_inc(sm["ev"].h, 1)
                        v.sem_inc(sm["cstV"].h, 1)
                        v.sem_inc(sm["stv"].h, 2)
                    else:
                        v.tensor_copy(wk[b2][:, :], ps[bank][:, :])
                        v.drain()
                        v.sem_inc(sm["ev"].h, 1)
                        v.wait_ge(sm["ssq_s"].h, g["sq"])
                        for hs in range(4):
                            v.tensor_reduce(rn[b2][:, hs:hs + 1], ssq[b2][:, hs * 128:(hs + 1) * 128], AX.X, ALU.add)
                        v.drain()
                        v.sem_inc(sm["red"].h, 1)
                        v.wait_ge(sm["rsq"].h, g["rsq"])
                        h0 = (g["piece"] % 4) * 4
                        for hs in range(4):
                            h = h0 + hs
                            if seg == 0:
                                v.tensor_scalar(wk[b2][:, hs * 128:(hs + 1) * 128], wk[b2][:, hs * 128:(hs + 1) * 128],
                                                rn[b2][:, hs:hs + 1], gainb[:, h:h + 1], ALU.mult, ALU.mult)
                            else:
                                v.tensor_scalar(wk[b2][:, hs * 128:(hs + 1) * 128], wk[b2][:, hs * 128:(hs + 1) * 128],
                                                rn[b2][:, hs:hs + 1], None, ALU.mult)
                        v.drain()
                        co = cosb[:, tc * 64:(tc + 1) * 64]
                        si = sinb[:, tc * 64:(tc + 1) * 64]
                        for hs in (0, 2):
                            sl = wk[b2][:, hs * 128:(hs + 1) * 128]
                            t1_, t2_ = sl[:, 0:64], sl[:, 64:128]
                            A, B = rtmp[:, 0:64], rtmp[:, 64:128]
                            C, D = rtmp[:, 128:192], rtmp[:, 192:256]
                            v.tensor_tensor(A, t1_, co, ALU.mult)
                            v.tensor_tensor(B, t2_, si, ALU.mult)
                            v.tensor_tensor(C, t1_, si, ALU.mult)
                            v.tensor_tensor(D, t2_, co, ALU.mult)
                            v.drain()
                            v.tensor_tensor(t1_, A, B, ALU.subtract)
                            v.tensor_tensor(t2_, C, D, ALU.add)
                            v.drain()
                        v.sem_inc(sm["app"].h, 1)
                        pg = g["piece"]
                        if tc == 0 and pg >= 2:
                            psem = "dstg0" if pg % 2 == 0 else "dstg1"
                            v.wait_ge(sm[psem].h, 16 * ((pg - 2) // 2 + 1))
                        for hs in range(4):
                            v.wait_ge(sm["tr2"].h, g["tr2"][hs])
                            v.tensor_copy(qks[pg % 2][:, hs * 1024 + tc * 128: hs * 1024 + (tc + 1) * 128],
                                          ptr[hs % 2][:, :]).then_inc(sm["evQ"].h, 1)

            @blk.scalar
            def _(s):
                for n, g in enumerate(G):
                    if g["seg"] == 2:
                        continue
                    b2 = n % 2
                    s.wait_ge(sm["ev"].h, g["ev"])
                    s.square(ssq[b2][:, :], wk[b2][:, :]).then_inc(sm["ssq_s"].h, 1)
                    s.wait_ge(sm["red"].h, g["red"])
                    s.activation(rn[b2][:, 0:4], rn[b2][:, 0:4], AF.Ln, bias=eps6[:, 0:1], scale=1.0)
                    s.drain()
                    s.activation(rn[b2][:, 0:4], rn[b2][:, 0:4], AF.Exp, bias=0.0, scale=half_neg[:, 0:1]).then_inc(sm["rsq"].h, 1)
                    s.wait_ge(sm["app"].h, g["app"])
                    s.copy(stg[b2][:, :], wk[b2][:, :]).then_inc(sm["cstS"].h, 1)

        shard_specs = [(wg_sh, 0, RWG_N, DM), (wu_sh, RWG_N, RWU_N, DM), (wd_sh, RWG_N + RWU_N, RWD_N, DI)]
        tiles = []
        for si, (src, base, nrows, width) in enumerate(shard_specs):
            for i in range((nrows + 127) // 128):
                tiles.append((si, i, min(128, nrows - i * 128), width))

        with nc.Block() as blk:
            @blk.sync
            def _(sync):
                sync.wait_ge(sm["dout0"].h, sm["dout0"].n)
                sync.wait_ge(sm["dout1"].h, sm["dout1"].n)
                jj = 0
                for k, (si, i, rr, width) in enumerate(tiles):
                    src = shard_specs[si][0]
                    nh = (width + DI // 2 - 1) // (DI // 2)
                    for hh in range(nh):
                        w0 = hh * (DI // 2)
                        wl = min(DI // 2, width - w0)
                        if jj >= 1:
                            sync.wait_ge(sm["rx"].h, jj)
                        sync.dma_start(wsh[:rr, :wl], src[i * 128:i * 128 + rr, w0:w0 + wl]).then_inc(sm["dx"].h, sm["dx"].inc(16))
                        jj += 1

            @blk.vector
            def _(v):
                ntot = sum((shard_specs[si][3] + DI // 2 - 1) // (DI // 2) for (si, i, rr, w) in tiles)
                base_dx = sm["dx"].n - 16 * ntot
                kk2 = 0
                for k, (si, i, rr, width) in enumerate(tiles):
                    nh = (width + DI // 2 - 1) // (DI // 2)
                    for hh in range(nh):
                        w0 = hh * (DI // 2)
                        wl = min(DI // 2, width - w0)
                        kk2 += 1
                        v.wait_ge(sm["dx"].h, base_dx + 16 * kk2)
                        if hh == 0:
                            v.tensor_reduce(rwcol[:rr, k:k + 1], wsh[:rr, :wl], AX.X, ALU.max,
                                            apply_absolute_value=True)
                        else:
                            v.tensor_reduce(rmcB[:rr, 0:1], wsh[:rr, :wl], AX.X, ALU.max,
                                            apply_absolute_value=True)
                            v.drain()
                            v.tensor_tensor(rwcol[:rr, k:k + 1], rwcol[:rr, k:k + 1], rmcB[:rr, 0:1], ALU.max)
                        v.drain()
                        v.sem_inc(sm["rx"].h, 1)
                        sm["rx"].inc()

            @blk.scalar
            def _(s):
                s.wait_ge(sm["rx"].h, sm["rx"].n)
                for k, (si, i, rr, width) in enumerate(tiles):
                    base = shard_specs[si][1]
                    s.dma_start(rw_loc[base + i * 128: base + i * 128 + rr],
                                rwcol[:rr, k:k + 1]).then_inc(sm["drow"].h, sm["drow"].inc(16))
                s.wait_ge(sm["drow"].h, sm["drow"].n)
                s.dma_start(rwrow[:, :], bass.AP(rw_loc.tensor if hasattr(rw_loc, "tensor") else rw_loc,
                                                 0, [[13, 128], [1, 13]])).then_inc(sm["drow"].h, sm["drow"].inc(16))

        st1.close()
        stY = contextlib.ExitStack()
        y_g = [stY.enter_context(nc.sbuf_tensor(f"y_g{cc}", [128, 2048], BF16)) for cc in range(NCORE)]
        stBN = contextlib.ExitStack()
        bnc = [stBN.enter_context(nc.sbuf_tensor(f"bnc{i}", [128, 4096], BF16)) for i in range(2)]
        stQKV = contextlib.ExitStack()
        qk_g = [stQKV.enter_context(nc.sbuf_tensor(f"qk_g{cc}", [128, 4096], BF16)) for cc in range(NCORE)]
        v_g = [stQKV.enter_context(nc.sbuf_tensor(f"v_g{cc}", [128, 2048], BF16)) for cc in range(NCORE)]

        ps1_after_x1 = [0]
        with nc.Block() as blk:
            @blk.gpsimd
            def _(g):
                drow_total = sm["drow"].n
                for arm in nc.switch_core_id([g], n=128):
                    hd, pnc = int(arm) // 8, int(arm) % 8
                    if pnc not in PNC_MAP:
                        continue
                    me = PNC_MAP.index(pnc)
                    rid = HOST_RID[hd]
                    kk_rem = [warm_preps[0] + 1]
                    cnt = {"lsnd0": 0, "lsnd1": 0, "lself0": 0, "lself1": 0, "lrw": 0}

                    def send(dst_ap, src_ap, rsem, dj, sself, srem):
                        if dj == me:
                            cnt[sself] += 1
                            g.dma_start(dst_ap, src_ap).then_inc(sm[sself].h, 16)
                        else:
                            cnt[srem] += 1
                            g.remote_dma(dst_ap, src_ap, rsem, sm[srem].h,
                                         pid=PNC_MAP[dj], routing_id=rid,
                                         dma_engine_mask=RMASK).then_inc(sm["ps1"].h, 1)
                            g.wait_ge(sm["ps1"].h, kk_rem[0])
                            kk_rem[0] += 1
                            g.trigger_dma(1)

                    g.wait_ge(sm["dstg0"].h, 16 * (NPIECE // 2))
                    g.wait_ge(sm["dstg1"].h, 16 * (NPIECE // 2))
                    g.wait_ge(sm["drow"].h, drow_total)
                    for dj in range(NCORE):
                        send(rws_g[:, me * 13:(me + 1) * 13], rwrow[:, :], sm["rrw"].h, dj,
                             "lrw", "lrw")
                    g.wait_ge(sm["rrw"].h, 7 * 8)
                    piece_counts = []
                    for pg in range(12):
                        par = pg % 2
                        if pg >= 2:
                            pls, psf = piece_counts[pg - 2]
                            if pls:
                                g.wait_ge(sm[f"lsnd{par}"].h, 16 * pls)
                            if psf:
                                g.wait_ge(sm[f"lself{par}"].h, 16 * psf)
                        bsem = "dbn0" if par == 0 else "dbn1"
                        g.dma_start(bnc[par][:, :], stg_d[pg, :, :]).then_inc(sm[bsem].h, 16)
                        g.wait_ge(sm[bsem].h, 16 * (pg // 2 + 1))
                        if pg < 8:
                            isq = pg < 4
                            for hs in range(4):
                                h = (pg % 4) * 4 + hs
                                dj = h // 2
                                coloff = (0 if isq else 2048) + (h % 2) * 1024
                                send(qk_g[me][:, coloff:coloff + 1024],
                                     bnc[par][:, hs * 1024:(hs + 1) * 1024], sm["rqk"].h, dj,
                                     f"lself{par}", f"lsnd{par}")
                        else:
                            pr = pg - 8
                            for djl in range(2):
                                dj = 2 * pr + djl
                                for ch in range(2):
                                    send(v_g[me][:, ch * 1024:(ch + 1) * 1024],
                                         bnc[par][:, djl * 2048 + ch * 1024: djl * 2048 + (ch + 1) * 1024],
                                         sm["rv"].h, dj,
                                         f"lself{par}", f"lsnd{par}")
                        piece_counts.append((cnt[f"lsnd{par}"], cnt[f"lself{par}"]))
                    for par in range(2):
                        if cnt[f"lsnd{par}"]:
                            g.wait_ge(sm[f"lsnd{par}"].h, 16 * cnt[f"lsnd{par}"])
                        if cnt[f"lself{par}"]:
                            g.wait_ge(sm[f"lself{par}"].h, 16 * cnt[f"lself{par}"])
                    g.wait_ge(sm["lrw"].h, 16 * 8)
                    g.sem_inc(sm["xdone"].h, 1)
                    ps1_after_x1[0] = kk_rem[0] - 1


        with nc.Block() as blk:
            @blk.sync
            def _(sync):
                sync.wait_ge(sm["lrw"].h, 16 * 8)
                sync.wait_ge(sm["rrw"].h, 7 * 8)
                for cc in range(NCORE):
                    sync.dma_start(rw_full[cc * RWROW:(cc + 1) * RWROW],
                                   bass.AP(rws_g.tensor if hasattr(rws_g, "tensor") else rws_g,
                                           cc * 13, [[NCORE * 13, 128], [1, 13]])).then_inc(sm["rwa"].h, sm["rwa"].inc(16))
                sync.wait_ge(sm["rwa"].h, sm["rwa"].n)


        st2 = contextlib.ExitStack()
        e2 = st2.enter_context
        msk = e2(nc.sbuf_tensor("msk", [128, 2048], BF16))
        attn = [e2(nc.sbuf_tensor(f"attn{i}", [128, 1024], BF16)) for i in range(2)]
        acc = e2(nc.sbuf_tensor("acc", [128, 1024], BF16))
        dn512 = e2(nc.sbuf_tensor("dn512", [1, 512], F32))
        rcb = e2(nc.sbuf_tensor("rcb", [1, 512], BF16))
        ones_col = e2(nc.sbuf_tensor("ones_col", [128, 1], BF16))
        ones_row = e2(nc.sbuf_tensor("ones_row", [1, 128], BF16))
        rnorm = e2(nc.sbuf_tensor("rnorm", [128, 512], F32))
        y_stag = e2(nc.sbuf_tensor("y_stag", [128, HPC * S], BF16))
        psqk = [e2(nc.psum_tensor(f"psqk{i}", [128, 1024], F32)) for i in range(2)]
        ps_y = [e2(nc.psum_tensor(f"ps_y{i}", [128, 512], F32)) for i in range(2)]
        ps_d = e2(nc.psum_tensor("ps_d", [1, 512], F32))
        ps_b = e2(nc.psum_tensor("ps_b", [128, 512], F32))

        def q_ap(h, qb):
            cc = qb // 2
            return qk_g[cc][:, (h % 2) * 1024 + (qb % 2) * 512: (h % 2) * 1024 + (qb % 2) * 512 + 512]

        def k_ap(h, kb):
            cc = kb // 8
            return qk_g[cc][:, 2048 + (h % 2) * 1024 + (kb % 8) * 128: 2048 + (h % 2) * 1024 + (kb % 8) * 128 + 128]

        def v_ap(h, kb):
            cc = kb // 8
            return v_g[cc][:, (kb % 8) * 256 + (h % 2) * 128: (kb % 8) * 256 + (h % 2) * 128 + 128]

        with nc.Block() as blk:
            @blk.vector
            def _(v):
                v.memset(ones_col[:, :], 1.0)
                v.memset(ones_row[:, :], 1.0)
            @blk.sync
            def _(sync):
                sync.dma_start(msk[:, 0:1024], m01[:, :]).then_inc(sm["din2"].h, sm["din2"].inc(16))
                sync.dma_start(msk[:, 1024:2048], m23[:, :]).then_inc(sm["din2"].h, sm["din2"].inc(16))
                sync.wait_ge(sm["rqk"].h, 7 * 4 * 8)
                sync.wait_ge(sm["rv"].h, 7 * 2 * 8)
                sync.wait_ge(sm["xdone"].h, 1)

        with nc.Block() as blk:
            @blk.tensor
            def _(t):
                t.wait_ge(sm["din2"].h, 32)
                for nhq, d in enumerate(HQ):
                    h, qb = d["h"], d["qb"]
                    ng = len(d["grps"])
                    for gi, grp in enumerate(d["grps"]):
                        pp = psqk[gi % 2]
                        if grp["mm"] > 2:
                            t.wait_ge(sm["expd"].h, grp["mm"] - 2)
                        for j in range(2):
                            kb = gi * 2 + j
                            m = t.matmul(pp[:, j * 512:(j + 1) * 512], k_ap(h, kb), q_ap(h, qb),
                                         start=True, stop=True)
                            if j == 1:
                                m.then_inc(sm["mm2"].h, 1)
                        if gi >= ng - 2:
                            t.wait_ge(sm["accd"].h, grp["acc"])
                        else:
                            t.wait_ge(sm["expd"].h, grp["exp"])
                        if gi == 0 and nhq >= 2:
                            t.wait_ge(sm["yst"].h, nhq - 1)
                        for j in range(2):
                            kb = gi * 2 + j
                            m = t.matmul(ps_y[nhq % 2][:, :], v_ap(h, kb), attn[gi % 2][:, j * 512:(j + 1) * 512],
                                         start=(gi == 0 and j == 0), stop=(gi == ng - 1 and j == 1))
                            if j == 1:
                                m.then_inc(sm["avd"].h, 1)
                    t.wait_ge(sm["accd"].h, d["grps"][-1]["acc"])
                    if nhq >= 1:
                        t.wait_ge(sm["rcpd"].h, nhq)
                    t.matmul(ps_d[:, :], ones_col[:, :], acc[:, 0:512], start=True, stop=False)
                    t.matmul(ps_d[:, :], ones_col[:, :], acc[:, 512:1024], start=False, stop=True).then_inc(sm["dnd"].h, 1)
                    t.wait_ge(sm["rcpd"].h, d["rcp"])
                    t.matmul(ps_b[:, :], ones_row[:, :], rcb[:, :], start=True, stop=True).then_inc(sm["bcd"].h, 1)

            @blk.scalar
            def _(s):
                flat_grps = [g2 for d2 in HQ for g2 in d2["grps"]]
                gidx = 0
                for d in HQ:
                    for gi, grp in enumerate(d["grps"]):
                        s.wait_ge(sm["mm2"].h, grp["mm"])
                        if grp["mm"] > 2:
                            s.wait_ge(sm["avd"].h, grp["mm"] - 2)
                            s.wait_ge(sm["accd"].h, flat_grps[gidx - 2]["acc"])
                        s.activation(attn[gi % 2][:, :], psqk[gi % 2][:, :], AF.Exp,
                                     bias=0.0, scale=1.0).then_inc(sm["expd"].h, 1)
                        gidx += 1
                    s.wait_ge(sm["bcd"].h, d["bc"])
                    s.copy(rnorm[:, :], ps_b[:, :]).then_inc(sm["rnd"].h, 1)

            @blk.vector
            def _(v):
                for nhq, d in enumerate(HQ):
                    h, qb = d["h"], d["qb"]
                    ng = len(d["grps"])
                    v.memset(acc[:, :], 0)
                    v.drain()
                    for gi, grp in enumerate(d["grps"]):
                        v.wait_ge(sm["expd"].h, grp["exp"])
                        if gi >= ng - 2:
                            mslice = msk[:, 0:1024] if gi == ng - 2 else msk[:, 1024:2048]
                            v.tensor_tensor(attn[gi % 2][:, :], attn[gi % 2][:, :], mslice, ALU.mult)
                            v.drain()
                        v.tensor_tensor(acc[:, :], acc[:, :], attn[gi % 2][:, :], ALU.add)
                        v.drain()
                        v.sem_inc(sm["accd"].h, 1)
                    v.wait_ge(sm["dnd"].h, d["dn"])
                    v.reciprocal(dn512[:, :], ps_d[:, :])
                    v.drain()
                    v.tensor_copy(rcb[:, :], dn512[:, :])
                    v.drain()
                    v.sem_inc(sm["rcpd"].h, 1)
                    v.wait_ge(sm["avd"].h, d["grps"][-1]["av"])
                    v.wait_ge(sm["rnd"].h, d["yn"])
                    v.tensor_tensor(y_stag[:, h * S + qb * 512: h * S + (qb + 1) * 512],
                                    ps_y[nhq % 2][:, :], rnorm[:, :], ALU.mult)
                    v.drain()
                    v.sem_inc(sm["yst"].h, 1)

        with nc.Block() as blk:
            @blk.gpsimd
            def _(g):
                for arm in nc.switch_core_id([g], n=128):
                    hd, pnc = int(arm) // 8, int(arm) % 8
                    if pnc not in PNC_MAP:
                        continue
                    me = PNC_MAP.index(pnc)
                    rid = HOST_RID[hd]
                    kk_rem = [ps1_after_x1[0] + 1]
                    for h in range(HPC):
                        for qb in range(NQB):
                            dj = qb // 2
                            src = y_stag[:, h * S + qb * 512: h * S + (qb + 1) * 512]
                            dst = y_g[me][:, h * 1024 + (qb % 2) * 512: h * 1024 + (qb % 2) * 512 + 512]
                            if dj == me:
                                g.dma_start(dst, src).then_inc(sm["lself2"].h, 16)
                            else:
                                g.remote_dma(dst, src, sm["ry"].h, sm["lsnd2"].h,
                                             pid=PNC_MAP[dj], routing_id=rid,
                                             dma_engine_mask=RMASK).then_inc(sm["ps1"].h, 1)
                                g.wait_ge(sm["ps1"].h, kk_rem[0])
                                kk_rem[0] += 1
                                g.trigger_dma(1)
                g.wait_ge(sm["lsnd2"].h, 16 * 28)
                g.wait_ge(sm["lself2"].h, 16 * 4)

        st2.close()
        stQKV.close()
        stBN.close()
        stR2.close()

        st3 = contextlib.ExitStack()
        e3 = st3.enter_context
        xw = e3(nc.sbuf_tensor("xw", [128, DM], F32))
        rm_sc = e3(nc.sbuf_tensor("rm_sc", [128, 2 * NTC], F32))
        rmt_raw = e3(nc.sbuf_tensor("rmt_raw", [128, NTC], F32))
        rmt_sc = e3(nc.sbuf_tensor("rmt_sc", [128, 2 * NTC], F32))
        rmc = e3(nc.sbuf_tensor("rmc", [128, 2], F32))
        ws = [e3(nc.sbuf_tensor(f"ws{i}", [128, 1024], F32)) for i in range(2)]
        qwf = e3(nc.sbuf_tensor("qwf", [128, NP2 * 1024], FP8))
        rowp = e3(nc.sbuf_tensor("rowp", [1, 1024], F32))
        binv = e3(nc.sbuf_tensor("binv", [128, 512], F32, side="right"))
        boutg = [e3(nc.sbuf_tensor(f"boutg{i}", [128, 512], F32, side="right")) for i in range(2)]
        boutu = [e3(nc.sbuf_tensor(f"boutu{i}", [128, 512], F32, side="right")) for i in range(2)]
        tseg = [e3(nc.sbuf_tensor(f"tseg{i}", [128, 512], F32)) for i in range(2)]
        y2 = e3(nc.sbuf_tensor("y2", [128, NTC * DM], BF16))
        ps3 = [e3(nc.psum_tensor(f"ps3_{i}", [128, 512], F32)) for i in range(7)]

        def dr_lhsT(tens, width, base):
            return bass.AP(tens, base, [[width, 128], [128, 2], [1, 128]])

        def dr_rhs(slot):
            return bass.AP(qwf, slot * 1024, [[NP2 * 1024, 128], [512, 2], [1, 512]])

        stA = contextlib.ExitStack()
        qa = stA.enter_context(nc.sbuf_tensor("qa", [128, DM], FP8))
        qaT_t = stA.enter_context(nc.sbuf_tensor("qaT", [128, NTC * DM], FP8))
        gbuf = stA.enter_context(nc.sbuf_tensor("gbuf", [128, NTC * 512], F32))
        sil = stA.enter_context(nc.sbuf_tensor("sil", [128, 512], F32))
        ubuf = stA.enter_context(nc.sbuf_tensor("ubuf", [128, 512], F32))
        stPA = contextlib.ExitStack()
        pst = stPA.enter_context(nc.psum_tensor("pstA", [128, 256], FP8))
        ps8holder = {}
        def PS3B(tc):
            return ps3[tc] if tc < 7 else ps8holder["t"]

        with nc.Block() as blk:
            @blk.sync
            def _(sync):
                for tc in range(NTC):
                    if tc >= 1:
                        sync.wait_ge(sm["vA"].h, 3 * tc)
                    sync.dma_start(xw[:, :], xn_d[tc * 128:(tc + 1) * 128, :]).then_inc(sm["drp"].h, sm["drp"].inc(16))

            @blk.vector
            def _(v):
                for tc in range(NTC):
                    v.wait_ge(sm["drp"].h, 16 * (tc + 1))
                    v.tensor_reduce(rmc[:, 0:1], xw[:, :], AX.X, ALU.max, apply_absolute_value=True)
                    v.drain()
                    v.reciprocal(rmc[:, 1:2], rmc[:, 0:1])
                    v.drain()
                    v.tensor_scalar(rm_sc[:, 2 * tc:2 * tc + 1], rmc[:, 1:2], 224.0, None, ALU.mult)
                    v.tensor_scalar(rm_sc[:, 2 * tc + 1:2 * tc + 2], rmc[:, 0:1], Q2C, None, ALU.mult)
                    v.drain()
                    v.tensor_scalar(qa[:, :], xw[:, :], rm_sc[:, 2 * tc:2 * tc + 1], None, ALU.mult)
                    v.drain()
                    v.sem_inc(sm["vA"].h, 3)
                    for f in range(NDC):
                        v.wait_ge(sm["tA"].h, tc * NDC + f + 1)
                        fp, i = divmod(f, 2)
                        v.tensor_copy(qaT_t[:, tc * DM + fp * 256 + i * 128: tc * DM + fp * 256 + (i + 1) * 128],
                                      bass.AP(pst, 0, [[256, 128], [2, 128]])).then_inc(sm["evA"].h, 1)

            @blk.tensor
            def _(t):
                t.wait_ge(sm["did"].h, 16)
                for tc in range(NTC):
                    t.wait_ge(sm["vA"].h, 3 * tc + 3)
                    for f in range(NDC):
                        n = tc * NDC + f
                        if n >= 1:
                            t.wait_ge(sm["evA"].h, n)
                        t.transpose(bass.AP(pst, 0, [[256, 128], [2, 128]]),
                                    qa[:, f * 128:(f + 1) * 128], id8b[:, :]).then_inc(sm["tA"].h, 1)

        stPA.close()
        stPB = contextlib.ExitStack()
        ps8holder["t"] = stPB.enter_context(nc.psum_tensor("ps8b", [128, 512], F32))

        with nc.Block() as blk:
            @blk.sync
            def _(sync):
                for d in SCH:
                    ip = d["ip"]
                    if ip >= 1:
                        sync.wait_ge(sm["bcG"].h, ip)
                    pos = 0
                    for off, ln in rw_piece_dmas("g", ip * 512, 512):
                        sync.dma_start(bass.AP(rowp.tensor if hasattr(rowp, "tensor") else rowp, pos, [[1024, 1], [1, ln]]),
                                       rw_full[off:off + ln]).then_inc(sm["drp"].h, sm["drp"].inc(16))
                        pos += ln
                    pos = 512
                    for off, ln in rw_piece_dmas("u", ip * 512, 512):
                        sync.dma_start(bass.AP(rowp.tensor if hasattr(rowp, "tensor") else rowp, pos, [[1024, 1], [1, ln]]),
                                       rw_full[off:off + ln]).then_inc(sm["drp"].h, sm["drp"].inc(16))
                        pos += ln

            @blk.gpsimd
            def _(g):
                for d in SCH:
                    ip = d["ip"]
                    g.wait_ge(sm["drp"].h, 128 + d["rp"])
                    if ip > 1:
                        g.wait_ge(sm["tdB"].h, SCH[ip - 2]["td"][-1])
                    g.partition_broadcast(boutg[ip % 2][:, :], rowp[0:1, 0:512], channels=128)
                    g.partition_broadcast(boutu[ip % 2][:, :], rowp[0:1, 512:1024], channels=128)
                    g.drain()
                    g.sem_inc(sm["bcG"].h, 1)
                    for w, src in (("g", wgT), ("u", wuT)):
                        for fp in range(NFP):
                            buf = fp % 2
                            qm = d["qw" + w][fp]
                            if qm - 2 >= 1:
                                g.wait_ge(sm["qwB"].h, qm - 2)
                            dsn, dsv = d["dw" + w][fp]
                            g.dma_start(ws[buf][:, 0:512],
                                        src[fp * 256:fp * 256 + 128, ip * 512:(ip + 1) * 512]).then_inc(sm[dsn].h, 16)
                            g.dma_start(ws[buf][:, 512:1024],
                                        src[fp * 256 + 128:fp * 256 + 256, ip * 512:(ip + 1) * 512]).then_inc(sm[dsn].h, 16)

            @blk.vector
            def _(v):
                for d in SCH:
                    ip = d["ip"]
                    v.wait_ge(sm["bcG"].h, ip + 1)
                    for w in ("g", "u"):
                        if w == "u":
                            v.wait_ge(sm["mmB"].h, d["mmg"][-1])
                        elif ip > 0:
                            v.wait_ge(sm["mmB"].h, SCH[ip - 1]["mmu"][-1])
                        bout = boutg[ip % 2] if w == "g" else boutu[ip % 2]
                        v.reciprocal(binv[:, :], bout[:, :])
                        v.drain()
                        v.tensor_scalar(binv[:, :], binv[:, :], 224.0, None, ALU.mult)
                        v.drain()
                        for fp in range(NFP):
                            buf = fp % 2
                            dsn, dsv = d["dw" + w][fp]
                            v.wait_ge(sm[dsn].h, dsv)
                            v.tensor_tensor(qwf[:, fp * 1024:fp * 1024 + 512], ws[buf][:, 0:512], binv[:, :], ALU.mult)
                            v.tensor_tensor(qwf[:, fp * 1024 + 512:(fp + 1) * 1024], ws[buf][:, 512:1024], binv[:, :], ALU.mult)
                            v.drain()
                            v.sem_inc(sm["qwB"].h, 1)
                    for tc in range(NTC):
                        v.wait_ge(sm["mmB"].h, d["mmg"][tc])
                        v.scalar_tensor_tensor(gbuf[:, tc * 512:(tc + 1) * 512], PS3B(tc)[:, :],
                                               rm_sc[:, 2 * tc + 1:2 * tc + 2], boutg[ip % 2][:, :],
                                               ALU.mult, ALU.mult).then_inc(sm["evB"].h, 1)
                    for tc in range(NTC):
                        v.wait_ge(sm["mmB"].h, d["mmu"][tc])
                        v.scalar_tensor_tensor(ubuf[:, :], PS3B(tc)[:, :],
                                               rm_sc[:, 2 * tc + 1:2 * tc + 2], boutu[ip % 2][:, :],
                                               ALU.mult, ALU.mult)
                        v.drain()
                        v.wait_ge(sm["silB"].h, d["sil"][tc])
                        n0 = d["td"][tc] - 1
                        if n0 >= 2:
                            psem = "dsp0" if n0 % 2 == 0 else "dsp1"
                            v.wait_ge(sm[psem].h, 16 * ((n0 - 2) // 2 + 1))
                        v.tensor_tensor(tseg[n0 % 2][:, :], sil[:, :], ubuf[:, :], ALU.mult)
                        v.drain()
                        if ip == 0:
                            v.tensor_reduce(rmt_raw[:, tc:tc + 1], tseg[n0 % 2][:, :], AX.X, ALU.max,
                                            apply_absolute_value=True)
                        else:
                            v.tensor_reduce(rmc[:, 1:2], tseg[n0 % 2][:, :], AX.X, ALU.max,
                                            apply_absolute_value=True)
                            v.drain()
                            v.tensor_tensor(rmt_raw[:, tc:tc + 1], rmt_raw[:, tc:tc + 1], rmc[:, 1:2], ALU.max)
                        v.drain()
                        v.sem_inc(sm["tdB"].h, 1)
                v.reciprocal(rmt_sc[:, 0:NTC], rmt_raw[:, :])
                v.drain()
                v.tensor_scalar(rmt_sc[:, 0:NTC], rmt_sc[:, 0:NTC], 224.0, None, ALU.mult)
                v.tensor_scalar(rmt_sc[:, NTC:2 * NTC], rmt_raw[:, :], Q2C, None, ALU.mult)
                v.drain()
                v.sem_inc(sm["vC"].h, 1)

            @blk.scalar
            def _(s):
                nsil = 0
                for d in SCH:
                    for tc in range(NTC):
                        s.wait_ge(sm["evB"].h, d["ev"][tc])
                        if nsil >= 1:
                            s.wait_ge(sm["tdB"].h, nsil)
                        s.activation(sil[:, :], gbuf[:, tc * 512:(tc + 1) * 512], AF.Silu,
                                     bias=0.0, scale=1.0).then_inc(sm["silB"].h, 1)
                        nsil += 1
                        n0 = d["td"][tc] - 1
                        s.wait_ge(sm["tdB"].h, d["td"][tc])
                        psem = "dsp0" if n0 % 2 == 0 else "dsp1"
                        s.dma_start(t_spill[tc * 128:(tc + 1) * 128, d["ip"] * 512:(d["ip"] + 1) * 512],
                                    tseg[n0 % 2][:, :]).then_inc(sm[psem].h, 16)

            @blk.tensor
            def _(t):
                for d in SCH:
                    ip = d["ip"]
                    for w in ("g", "u"):
                        for tc in range(NTC):
                            if w == "g" and ip > 0:
                                t.wait_ge(sm["tdB"].h, SCH[ip - 1]["td"][tc])
                            if w == "u":
                                t.wait_ge(sm["evB"].h, d["ev"][tc])
                            for fp in range(NFP):
                                t.wait_ge(sm["qwB"].h, d["qw" + w][fp])
                                m = t.matmul(PS3B(tc)[:, :],
                                             dr_lhsT(qaT_t, NTC * DM, tc * DM + fp * 256),
                                             dr_rhs(fp),
                                             start=(fp == 0), stop=(fp == NFP - 1),
                                             perf_mode=DR)
                                if fp == NFP - 1:
                                    m.then_inc(sm["mmB"].h, 1)

        stA.close()

        stPB.close()
        stPC = contextlib.ExitStack()
        pst2 = stPC.enter_context(nc.psum_tensor("pstC", [128, 256], FP8))
        stC = contextlib.ExitStack()
        tw = stC.enter_context(nc.sbuf_tensor("tw", [128, DI], F32))
        qt_t = stC.enter_context(nc.sbuf_tensor("qt", [128, DI], FP8))
        qtT_t = stC.enter_context(nc.sbuf_tensor("qtT", [128, NTC * DI], FP8))

        with nc.Block() as blk:
            @blk.sync
            def _(sync):
                nspill = NIP * NTC
                sync.wait_ge(sm["dsp0"].h, 16 * ((nspill + 1) // 2))
                sync.wait_ge(sm["dsp1"].h, 16 * (nspill // 2))
                for tc in range(NTC):
                    if tc >= 1:
                        sync.wait_ge(sm["vC"].h, 1 + tc)
                    sync.dma_start(tw[:, :], t_spill[tc * 128:(tc + 1) * 128, :]).then_inc(sm["dtC"].h, 16)

            @blk.vector
            def _(v):
                for tc in range(NTC):
                    v.wait_ge(sm["dtC"].h, 16 * (tc + 1))
                    v.tensor_scalar(qt_t[:, :], tw[:, :], rmt_sc[:, tc:tc + 1], None, ALU.mult)
                    v.drain()
                    v.sem_inc(sm["vC"].h, 1)
                    for ic in range(DI // 128):
                        v.wait_ge(sm["tC"].h, tc * (DI // 128) + ic + 1)
                        ip2, i = divmod(ic, 2)
                        v.tensor_copy(qtT_t[:, tc * DI + ip2 * 256 + i * 128: tc * DI + ip2 * 256 + (i + 1) * 128],
                                      bass.AP(pst2, 0, [[256, 128], [2, 128]])).then_inc(sm["evC"].h, 1)

            @blk.tensor
            def _(t):
                for tc in range(NTC):
                    t.wait_ge(sm["vC"].h, 2 + tc)
                    for ic in range(DI // 128):
                        n = tc * (DI // 128) + ic
                        if n >= 1:
                            t.wait_ge(sm["evC"].h, n)
                        t.transpose(bass.AP(pst2, 0, [[256, 128], [2, 128]]),
                                    qt_t[:, ic * 128:(ic + 1) * 128], id8b[:, :]).then_inc(sm["tC"].h, 1)

        stPC.close()
        stPD = contextlib.ExitStack()
        ps8holder["t"] = stPD.enter_context(nc.psum_tensor("ps8d", [128, 512], F32))

        DRP_BASE_D = [sm["drp"].n]
        with nc.Block() as blk:
            @blk.sync
            def _(sync):
                for d in SD:
                    hp = d["hp"]
                    if hp >= 1:
                        sync.wait_ge(sm["bcD"].h, hp)
                    pos = 0
                    for off, ln in rw_piece_dmas("d", hp * 512, 512):
                        sync.dma_start(bass.AP(rowp, pos, [[1024, 1], [1, ln]]),
                                       rw_full[off:off + ln]).then_inc(sm["drp"].h, sm["drp"].inc(16))
                        pos += ln

            @blk.gpsimd
            def _(g):
                base = DRP_BASE_D[0]
                for d in SD:
                    hp = d["hp"]
                    g.wait_ge(sm["drp"].h, base + d["rp"])
                    if hp > 1:
                        g.wait_ge(sm["evD"].h, SD[hp - 2]["ev"][-1])
                    g.partition_broadcast(boutg[hp % 2][:, :], rowp[0:1, 0:512], channels=128)
                    g.drain()
                    g.sem_inc(sm["bcD"].h, 1)
                    for ip2 in range(NP2):
                        buf = ip2 % 2
                        qm = d["qw"][ip2]
                        if qm - 2 >= 1:
                            g.wait_ge(sm["qwD"].h, qm - 2)
                        dsn, dsv = d["dw"][ip2]
                        g.dma_start(ws[buf][:, 0:512],
                                    wd2T[ip2 * 256:ip2 * 256 + 128, hp * 512:(hp + 1) * 512]).then_inc(sm[dsn].h, 16)
                        g.dma_start(ws[buf][:, 512:1024],
                                    wd2T[ip2 * 256 + 128:ip2 * 256 + 256, hp * 512:(hp + 1) * 512]).then_inc(sm[dsn].h, 16)

            @blk.vector
            def _(v):
                for d in SD:
                    hp = d["hp"]
                    v.wait_ge(sm["bcD"].h, hp + 1)
                    if hp > 0:
                        v.wait_ge(sm["mmD"].h, SD[hp - 1]["mm"][-1])
                    v.reciprocal(binv[:, :], boutg[hp % 2][:, :])
                    v.drain()
                    v.tensor_scalar(binv[:, :], binv[:, :], 224.0, None, ALU.mult)
                    v.drain()
                    for ip2 in range(NP2):
                        buf = ip2 % 2
                        dsn, dsv = d["dw"][ip2]
                        v.wait_ge(sm[dsn].h, dsv)
                        v.tensor_tensor(qwf[:, ip2 * 1024:ip2 * 1024 + 512], ws[buf][:, 0:512], binv[:, :], ALU.mult)
                        v.tensor_tensor(qwf[:, ip2 * 1024 + 512:(ip2 + 1) * 1024], ws[buf][:, 512:1024], binv[:, :], ALU.mult)
                        v.drain()
                        v.sem_inc(sm["qwD"].h, 1)
                    for tc in range(NTC):
                        v.wait_ge(sm["mmD"].h, d["mm"][tc])
                        v.scalar_tensor_tensor(y2[:, tc * DM + hp * 512: tc * DM + (hp + 1) * 512], PS3B(tc)[:, :],
                                               rmt_sc[:, NTC + tc:NTC + tc + 1], boutg[hp % 2][:, :],
                                               ALU.mult, ALU.mult)
                        v.drain()
                        v.sem_inc(sm["evD"].h, 1)

            @blk.tensor
            def _(t):
                for d in SD:
                    hp = d["hp"]
                    for tc in range(NTC):
                        if hp > 0:
                            t.wait_ge(sm["evD"].h, SD[hp - 1]["ev"][tc])
                        for ip2 in range(NP2):
                            t.wait_ge(sm["qwD"].h, d["qw"][ip2])
                            m = t.matmul(PS3B(tc)[:, :],
                                         dr_lhsT(qtT_t, NTC * DI, tc * DI + ip2 * 256),
                                         dr_rhs(ip2),
                                         start=(ip2 == 0), stop=(ip2 == NP2 - 1),
                                         perf_mode=DR)
                            if ip2 == NP2 - 1:
                                m.then_inc(sm["mmD"].h, 1)

        stC.close()

        stE = contextlib.ExitStack()
        wob = [stE.enter_context(nc.sbuf_tensor(f"wob{i}", [128, 512], BF16)) for i in range(2)]

        with nc.Block() as blk:
            @blk.gpsimd
            def _(g):
                g.wait_ge(sm["ry"].h, 7 * 4 * 8)
                g.wait_ge(sm["lself2"].h, 4 * 16)
                for op in range(NHP):
                    d = SE[op]
                    for dc in range(NDC):
                        n = op * NDC + dc
                        if n >= 2:
                            pn = n - 2
                            pd = SE[pn // NDC]
                            if pd["wu"][pn % NDC] is not None:
                                g.wait_ge(sm["bufE"].h, pd["wu"][pn % NDC])
                            else:
                                g.wait_ge(sm["evE"].h, pd["ev"][-1])
                        dsn, dsv = d["dw"][dc]
                        g.dma_start(wob[dc % 2][:, :],
                                    w_out[dc * 128:(dc + 1) * 128, op * 512:(op + 1) * 512]).then_inc(sm[dsn].h, 16)

            @blk.tensor
            def _(t):
                for op in range(NHP):
                    d = SE[op]
                    for dc in range(NDC):
                        dsn, dsv = d["dw"][dc]
                        t.wait_ge(sm[dsn].h, dsv)
                        for tc in range(NTC):
                            if dc == 0 and op > 0:
                                t.wait_ge(sm["evE"].h, SE[op - 1]["ev"][tc])
                            m = t.matmul(PS3B(tc)[:, :],
                                         y_g[dc // 2][:, (dc % 2) * 1024 + tc * 128: (dc % 2) * 1024 + (tc + 1) * 128],
                                         wob[dc % 2][:, :],
                                         start=(dc == 0), stop=(dc == NDC - 1))
                            if tc == NTC - 1 and dc < NDC - 1:
                                m.then_inc(sm["bufE"].h, 1)
                            if dc == NDC - 1:
                                m.then_inc(sm["mmE"].h, 1)

            @blk.vector
            def _(v):
                for op in range(NHP):
                    d = SE[op]
                    for tc in range(NTC):
                        v.wait_ge(sm["mmE"].h, d["mm"][tc])
                        v.tensor_tensor(y2[:, tc * DM + op * 512: tc * DM + (op + 1) * 512],
                                        y2[:, tc * DM + op * 512: tc * DM + (op + 1) * 512],
                                        PS3B(tc)[:, :], ALU.add)
                        v.drain()
                        v.sem_inc(sm["evE"].h, 1)

        stE.close()
        stPD.close()
        stF = contextlib.ExitStack()
        ostg = [stF.enter_context(nc.sbuf_tensor(f"ostg{i}", [128, DM], F32)) for i in range(2)]

        with nc.Block() as blk:
            @blk.sync
            def _(sync):
                for tc in range(NTC):
                    if tc >= 1:
                        sync.wait_ge(sm["vF"].h, tc)
                    sync.dma_start(xw[:, :], x[tc * 128:(tc + 1) * 128, :]).then_inc(sm["dxF"].h, 16)

            @blk.vector
            def _(v):
                for tc in range(NTC):
                    v.wait_ge(sm["dxF"].h, 16 * (tc + 1))
                    if tc >= 2:
                        v.wait_ge(sm["doF0" if tc % 2 == 0 else "doF1"].h, 16 * ((tc - 2) // 2 + 1))
                    v.tensor_tensor(ostg[tc % 2][:, :], xw[:, :], y2[:, tc * DM:(tc + 1) * DM], ALU.add)
                    v.drain()
                    v.sem_inc(sm["vF"].h, 1)

            @blk.gpsimd
            def _(g):
                for tc in range(NTC):
                    g.wait_ge(sm["vF"].h, tc + 1)
                    dsem = sm["doF0"] if tc % 2 == 0 else sm["doF1"]
                    g.dma_start(out_o[tc * 128:(tc + 1) * 128, :], ostg[tc % 2][:, :]).then_inc(dsem.h, dsem.inc(16))
                g.wait_ge(sm["doF0"].h, sm["doF0"].n)
                g.wait_ge(sm["doF1"].h, sm["doF1"].n)
        stF.close()
        st3.close()
        stY.close()
    return nc


_NC_CACHE = {}

def _get_nc(name, builder):
    if name not in _NC_CACHE:
        nc = builder()
        nc.compile()
        _NC_CACHE[name] = nc
    return _NC_CACHE[name]


def kernel(L0_attn_x, L0_attn_w_qkv, L0_attn_w_out, L0_attn_q_gain,
           L0_ffn_w_gate, L0_ffn_w_up, L0_ffn_w_down,
           L0_ffn_up_reorder_index, L0_ffn_down_reorder_index):
    import ml_dtypes
    from concourse.bass_utils import run_bass_kernel_spmd

    x_full = np.ascontiguousarray(np.asarray(L0_attn_x)[0])
    w_qkv = np.ascontiguousarray(np.asarray(L0_attn_w_qkv))
    w_out = np.ascontiguousarray(np.asarray(L0_attn_w_out))
    q_gain = np.asarray(L0_attn_q_gain)
    wg = np.ascontiguousarray(np.asarray(L0_ffn_w_gate))
    wu = np.ascontiguousarray(np.asarray(L0_ffn_w_up))
    wd = np.ascontiguousarray(np.asarray(L0_ffn_w_down))
    di_idx = np.asarray(L0_ffn_down_reorder_index)
    wd2 = np.ascontiguousarray(wd[:, np.argsort(di_idx)])

    half = 64
    inv_freq = (10000.0 ** (-(np.arange(half, dtype=np.float32)) / half)).astype(np.float32)
    theta = (np.arange(S, dtype=np.float32)[:, None] * inv_freq[None, :]).astype(np.float32)
    cos_full = np.cos(theta).astype(np.float32)
    sin_full = np.sin(theta).astype(np.float32)
    ident = np.eye(128, dtype=np.float32).astype(ml_dtypes.bfloat16)
    id8 = np.eye(128, dtype=np.float32).astype(ml_dtypes.float8_e4m3)
    gain_b = np.ascontiguousarray(np.broadcast_to(q_gain[None, :].astype(np.float32), (128, NH)))
    jj = np.arange(512)[None, :]
    tt = np.arange(128)[:, None]
    masks = [(jj >= j * 128 + tt).astype(np.float32) for j in range(4)]
    m01 = np.ascontiguousarray(np.concatenate(masks[0:2], 1)).astype(ml_dtypes.bfloat16)
    m23 = np.ascontiguousarray(np.concatenate(masks[2:4], 1)).astype(ml_dtypes.bfloat16)
    wgT_np = np.ascontiguousarray(wg.T)
    wuT_np = np.ascontiguousarray(wu.T)
    wd2T_np = np.ascontiguousarray(wd2.T)

    nc = _get_nc("fused", build_fused)
    ins = []
    for c in range(NCORE):
        ins.append({
            "x": np.ascontiguousarray(x_full[c * TPC:(c + 1) * TPC]),
            "w_qkv": w_qkv,
            "cos_t": np.ascontiguousarray(cos_full[c * TPC:(c + 1) * TPC]),
            "sin_t": np.ascontiguousarray(sin_full[c * TPC:(c + 1) * TPC]),
            "gain_b": gain_b,
            "ident": ident,
            "id8": id8,
            "wg_sh": np.ascontiguousarray(wg[c * RWG_N:(c + 1) * RWG_N]),
            "wu_sh": np.ascontiguousarray(wu[c * RWU_N:(c + 1) * RWU_N]),
            "wd_sh": np.ascontiguousarray(wd2[c * RWD_N:(c + 1) * RWD_N]),
            "m01": m01, "m23": m23,
            "w_out": w_out,
            "wgT": wgT_np, "wuT": wuT_np, "wd2T": wd2T_np,
        })
    r = run_bass_kernel_spmd(nc, ins, core_ids=list(range(NCORE)))
    out = np.concatenate([np.asarray(r.results[c]["out_o"]) for c in range(NCORE)], axis=0)
    return out[None, :, :].astype(np.float32)


# version 10 (final)
# speedup vs baseline: 1.4227x; 1.4227x over previous
import numpy as np
import contextlib

import numpy as np
import concourse.bass as bass
import concourse.bacc as bacc
import concourse.mybir as mybir

F32 = mybir.dt.float32
BF16 = mybir.dt.bfloat16
FP8 = mybir.dt.float8e4
I32 = mybir.dt.int32
AF = mybir.ActivationFunctionType
ALU = mybir.AluOpType
AX = mybir.AxisListType
DR = mybir.MatmulPerfMode.DoubleRow

NCORE = 8
S = 8192
DM = 2048
NH = 16
DH = 128
DI = 5632
TPC = S // NCORE
HPC = NH // NCORE

class Sem:
    def __init__(self, handle):
        self.h = handle
        self.n = 0
    def inc(self, k=1):
        self.n += k
        return k

def mk_sems(nc, names):
    import contextlib
    stack = contextlib.ExitStack()
    out = {}
    for n in names:
        out[n] = Sem(stack.enter_context(nc.semaphore(n)))
    return stack, out

class Counter:
    def __init__(self):
        self.n = 0
    def reserve(self, k=1):
        self.n += k
        return self.n


"""Fused single-launch prb depth-1 megakernel.

Per core: token-sharded LN+QKV prep (all heads, own 1024 tokens) ->
on-chip all-to-all (remote DMA) of transposed q/k, v, rowmax rows ->
head-sharded causal attention (2 heads, all tokens) -> y all-to-all ->
token-sharded FFN (fp8) + out-projection + residual.
"""

PNC_MAP = (4, 5, 6, 7, 2, 3, 0, 1)
HOST_RID = (0, 8, 9, 1, 2, 10, 11, 3, 6, 14, 15, 7, 4, 12, 13, 5)
RMASK = 0xf0f0

NTC = TPC // 128
NF = DM // 128
NPIECE = (3 * DM) // 512
NQB = S // 512
NIP = DI // 512
NFP = DM // 256
NP2 = DI // 256
NHP = DM // 512
NDC = DM // 128
Q2C = 4.0 / (448.0 * 448.0)
RWG_N, RWU_N, RWD_N = DI // 8, DI // 8, DM // 8
RWROW = RWG_N + RWU_N + RWD_N

def build_fused():
    nc = bacc.Bacc("TRN2")
    from concourse.library_config import proxy as proxy_lib

    x = nc.dram_tensor("x", [TPC, DM], F32, kind="ExternalInput").ap()
    w_qkv = nc.dram_tensor("w_qkv", [DM, 3 * DM], F32, kind="ExternalInput").ap()
    cos_t = nc.dram_tensor("cos_t", [TPC, 64], F32, kind="ExternalInput").ap()
    sin_t = nc.dram_tensor("sin_t", [TPC, 64], F32, kind="ExternalInput").ap()
    gain_b = nc.dram_tensor("gain_b", [128, NH], F32, kind="ExternalInput").ap()
    ident = nc.dram_tensor("ident", [128, 128], BF16, kind="ExternalInput").ap()
    id8 = nc.dram_tensor("id8", [128, 128], FP8, kind="ExternalInput").ap()
    wg_sh = nc.dram_tensor("wg_sh", [RWG_N, DM], F32, kind="ExternalInput").ap()
    wu_sh = nc.dram_tensor("wu_sh", [RWU_N, DM], F32, kind="ExternalInput").ap()
    wd_sh = nc.dram_tensor("wd_sh", [RWD_N, DI], F32, kind="ExternalInput").ap()
    m01 = nc.dram_tensor("m01", [128, 1024], BF16, kind="ExternalInput").ap()
    m23 = nc.dram_tensor("m23", [128, 1024], BF16, kind="ExternalInput").ap()
    w_out = nc.dram_tensor("w_out", [DM, DM], F32, kind="ExternalInput").ap()
    wgT = nc.dram_tensor("wgT", [DM, DI], F32, kind="ExternalInput").ap()
    wuT = nc.dram_tensor("wuT", [DM, DI], F32, kind="ExternalInput").ap()
    wd2T = nc.dram_tensor("wd2T", [DI, DM], F32, kind="ExternalInput").ap()
    out_o = nc.dram_tensor("out_o", [TPC, DM], F32, kind="ExternalOutput").ap()
    xn_d = nc.dram_tensor("xn_d", [TPC, DM], F32, kind="Internal").ap()
    t_spill = nc.dram_tensor("t_spill", [TPC, DI], F32, kind="Internal").ap()
    rw_loc = nc.dram_tensor("rw_loc", [RWROW], F32, kind="Internal").ap()
    stg_d = nc.dram_tensor("stg_d", [12, 128, 4096], BF16, kind="Internal").ap()
    rw_full = nc.dram_tensor("rw_full", [NCORE * RWROW], F32, kind="Internal").ap()

    cL = {k: Counter() for k in ["v", "s", "t", "evT"]}
    L = []
    for tc in range(NTC):
        d = {"xc": cL["v"].reserve(), "var": cL["v"].reserve(), "sqrt": cL["s"].reserve(),
             "xnr": cL["v"].reserve(), "xnb": cL["v"].reserve(),
             "tr": [cL["t"].reserve() for _ in range(NF)],
             "trev": [cL["evT"].reserve() for _ in range(NF)]}
        L.append(d)
    NCONST = 2 + 2 * NTC

    c = {k: Counter() for k in ["mm", "ev", "sq", "red", "rsq", "app", "cstV", "cstS",
                                "tr2", "evQ", "stv"]}
    G = []
    for piece in range(NPIECE):
        for tc in range(NTC):
            g = {"piece": piece, "tc": tc, "seg": piece // 4}
            g["mm"] = c["mm"].reserve()
            g["ev"] = c["ev"].reserve()
            if g["seg"] < 2:
                g["sq"] = c["sq"].reserve()
                g["red"] = c["red"].reserve()
                g["rsq"] = c["rsq"].reserve()
                g["app"] = c["app"].reserve()
                g["cst_sem"], g["cst"] = "cstS", c["cstS"].reserve()
                g["tr2"] = [c["tr2"].reserve() for _ in range(4)]
                g["evQ"] = [c["evQ"].reserve() for _ in range(4)]
            else:
                g["cst_sem"], g["cst"] = "cstV", c["cstV"].reserve()
                g["stv"] = [c["stv"].reserve() for _ in range(2)]
            G.append(g)
    def piece_groups(piece):
        return [G[piece * NTC + tc] for tc in range(NTC)]

    cA = {k: Counter() for k in ["mm", "exp", "acc", "av", "dn", "rcp", "bc", "yn", "yst"]}
    HQ = []
    for h in range(HPC):
        for qb in range(NQB):
            grps = [{"mm": cA["mm"].reserve(), "exp": cA["exp"].reserve(),
                     "acc": cA["acc"].reserve(), "av": cA["av"].reserve()}
                    for _ in range(2 * (qb + 1))]
            d = {"h": h, "qb": qb, "grps": grps, "dn": cA["dn"].reserve(),
                 "rcp": cA["rcp"].reserve(), "bc": cA["bc"].reserve(),
                 "yn": cA["yn"].reserve(), "yst": cA["yst"].reserve()}
            HQ.append(d)

    cdw = {0: Counter(), 1: Counter()}
    cqw = Counter(); cmm3 = Counter(); cev3 = Counter()
    csl = Counter(); ctd = Counter(); crp = Counter()
    SCH = []
    for ip in range(NIP):
        nd = 0
        def _rw_piece_dmas(kind, start, length):
            base = {"g": 0, "u": DI // 8 * 0 + RWG_N, "d": RWG_N + RWU_N}[kind]
            per = {"g": RWG_N, "u": RWU_N, "d": RWD_N}[kind]
            out = []
            pos = start
            while pos < start + length:
                cidx = pos // per
                off_in = pos - cidx * per
                ln = min(per - off_in, start + length - pos)
                out.append((cidx * RWROW + base + off_in, ln))
                pos += ln
            return out
        nd = len(_rw_piece_dmas("g", ip * 512, 512)) + len(_rw_piece_dmas("u", ip * 512, 512))
        d = {"ip": ip, "rp": crp.reserve(16 * nd)}
        for w in ("g", "u"):
            dws, qws = [], []
            for fp in range(NFP):
                par = fp % 2
                dws.append((f"dw{par}", cdw[par].reserve(32)))
                qws.append(cqw.reserve())
            d["dw" + w] = dws
            d["qw" + w] = qws
            d["mm" + w] = [cmm3.reserve() for _ in range(NTC)]
        d["ev"] = [cev3.reserve() for _ in range(NTC)]
        d["sil"] = [csl.reserve() for _ in range(NTC)]
        d["td"] = [ctd.reserve() for _ in range(NTC)]
        SCH.append(d)
    cqwD = Counter(); cmmD = Counter(); cevD = Counter(); crpD = Counter()
    SD = []
    for hp in range(NHP):
        ndd = 1 if (hp * 512) // RWD_N == (hp * 512 + 511) // RWD_N else 2
        d = {"hp": hp, "dw": [], "qw": [], "rp": crpD.reserve(16 * ndd)}
        for ip2 in range(NP2):
            par = ip2 % 2
            d["dw"].append((f"dw{par}", cdw[par].reserve(32)))
            d["qw"].append(cqwD.reserve())
        d["mm"] = [cmmD.reserve() for _ in range(NTC)]
        d["ev"] = [cevD.reserve() for _ in range(NTC)]
        SD.append(d)
    cmmE = Counter(); cevE = Counter(); cwu = Counter()
    cdE = {0: Counter(), 1: Counter()}
    SE = []
    for op in range(NHP):
        d = {"op": op, "dw": [], "wu": []}
        for dc in range(NDC):
            par = dc % 2
            d["dw"].append((f"dwE{par}", cdE[par].reserve(16)))
            d["wu"].append(cwu.reserve() if dc < NDC - 1 else None)
        d["mm"] = [cmmE.reserve() for _ in range(NTC)]
        d["ev"] = [cevE.reserve() for _ in range(NTC)]
        SE.append(d)

    def rw_piece_dmas(kind, start, length):
        base = {"g": 0, "u": RWG_N, "d": RWG_N + RWU_N}[kind]
        per = {"g": RWG_N, "u": RWU_N, "d": RWD_N}[kind]
        out = []
        pos = start
        while pos < start + length:
            cidx = pos // per
            off_in = pos - cidx * per
            ln = min(per - off_in, start + length - pos)
            out.append((cidx * RWROW + base + off_in, ln))
            pos += ln
        return out

    st = contextlib.ExitStack()
    e = st.enter_context
    with st:
        rws_g = e(nc.sbuf_tensor("rws_g", [128, NCORE * 13], F32))
        rwrow = e(nc.sbuf_tensor("rwrow", [128, 13], F32))
        eps5 = e(nc.sbuf_tensor("eps5", [128, 1], F32))
        eps6 = e(nc.sbuf_tensor("eps6", [128, 1], F32))
        half_neg = e(nc.sbuf_tensor("half_neg", [128, 1], F32))
        id8b = e(nc.sbuf_tensor("id8b", [128, 128], FP8))
        idn = e(nc.sbuf_tensor("idn", [128, 128], BF16))
        wrm = e(nc.sbuf_tensor("wrm", [128, 16], F32))
        wrs = e(nc.sbuf_tensor("wrs", [128, 2], F32))
        warm_preps = [0]

        names = ["dx", "did", "dw", "dwq", "dout0", "dout1", "vL", "sL", "tT", "evT",
                 "mm", "ev", "ssq_s", "red", "rsq", "app", "cstV", "cstS",
                 "tr2", "evQ", "stv", "rx", "drow",
                 "ps1", "rqk", "rv", "rrw", "lself0", "lself1", "lsnd0", "lsnd1", "lrw", "xdone", "dbn0", "dbn1", "dstg0", "dstg1",
                 "din2", "mm2", "expd", "accd", "avd", "dnd", "rcpd", "bcd", "rnd", "yst",
                 "ps2", "ry", "lself2", "lsnd2",
                 "drp", "vA", "tA", "evA",
                 "dw0", "dw1", "qwB", "mmB", "evB", "silB", "tdB", "dsp0", "dsp1", "bcG",
                 "dtC", "vC", "tC", "evC",
                 "qwD", "mmD", "evD", "bcD",
                 "dwE0", "dwE1", "bufE", "mmE", "evE",
                 "dxF", "vF", "doF0", "doF1", "rwa", "rwm", "lwm"]
        stk, sm = mk_sems(nc, names)
        e(stk)

        stR1 = contextlib.ExitStack()
        xs = stR1.enter_context(nc.sbuf_tensor("xs", [128, DM], F32, side="right"))
        xn = [stR1.enter_context(nc.sbuf_tensor(f"xn{i}", [128, DM], F32, side="right")) for i in range(2)]
        xc = stR1.enter_context(nc.sbuf_tensor("xc", [128, DM], F32, side="right"))
        sq_t = stR1.enter_context(nc.sbuf_tensor("sq_t", [128, DM], F32, side="right"))
        st1 = contextlib.ExitStack()
        e1 = st1.enter_context
        stat = e1(nc.sbuf_tensor("stat", [128, 4], F32))
        rstd_all = e1(nc.sbuf_tensor("rstd_all", [128, NTC], F32))
        xnb = e1(nc.sbuf_tensor("xnb", [128, DM], BF16))
        xnT = e1(nc.sbuf_tensor("xnT", [128, NTC * DM], BF16))
        cosb = e1(nc.sbuf_tensor("cosb", [128, NTC * 64], F32))
        sinb = e1(nc.sbuf_tensor("sinb", [128, NTC * 64], F32))
        gainb = e1(nc.sbuf_tensor("gainb", [128, NH], F32))
        wq = [e1(nc.sbuf_tensor(f"wqb{i}", [128, NF * 512], BF16)) for i in range(2)]
        wk = [e1(nc.sbuf_tensor(f"wk{i}", [128, 512], F32)) for i in range(2)]
        ssq = [e1(nc.sbuf_tensor(f"ssq{i}", [128, 512], F32)) for i in range(2)]
        rn = [e1(nc.sbuf_tensor(f"rn{i}", [128, 4], F32)) for i in range(2)]
        rtmp = e1(nc.sbuf_tensor("rtmp", [128, 256], F32))
        stg = [e1(nc.sbuf_tensor(f"stg{i}", [128, 512], BF16)) for i in range(2)]
        qks = [e1(nc.sbuf_tensor(f"qks{i}", [128, 4096], BF16)) for i in range(2)]
        vst = [e1(nc.sbuf_tensor(f"vst{i}", [128, 4096], BF16)) for i in range(2)]

        ptr = [e1(nc.psum_tensor(f"ptr{i}", [128, 128], BF16)) for i in range(2)]
        ps = [e1(nc.psum_tensor(f"ps{i}", [128, 512], F32)) for i in range(6)]

        with nc.Block() as blk:
            @blk.vector
            def _(v):
                v.memset(eps5[:, :], 1e-5)
                v.memset(eps6[:, :], 1e-6)
                v.memset(half_neg[:, :], -0.5)
                v.memset(wrm[:, :], 0.0)
                v.memset(wrs[:, :], 0.0)

        with nc.Block() as blk:
            @blk.sync
            def _(sync):
                sync.dma_start(idn[:, :], ident[:, :]).then_inc(sm["dx"].h, sm["dx"].inc(16))
                sync.dma_start(id8b[:, :], id8[:, :]).then_inc(sm["did"].h, sm["did"].inc(16))
                sync.dma_start(gainb[:, :], gain_b[:, :]).then_inc(sm["dx"].h, sm["dx"].inc(16))
                for tc in range(NTC):
                    sync.dma_start(cosb[:, tc * 64:(tc + 1) * 64],
                                   cos_t[tc * 128:(tc + 1) * 128, :]).then_inc(sm["dx"].h, sm["dx"].inc(16))
                    sync.dma_start(sinb[:, tc * 64:(tc + 1) * 64],
                                   sin_t[tc * 128:(tc + 1) * 128, :]).then_inc(sm["dx"].h, sm["dx"].inc(16))
                for tc in range(NTC):
                    if tc >= 1:
                        sync.wait_ge(sm["vL"].h, L[tc - 1]["xc"])
                    sync.dma_start(xs[:, :], x[tc * 128:(tc + 1) * 128, :]).then_inc(sm["dx"].h, sm["dx"].inc(16))

            @blk.vector
            def _(v):
                v.wait_ge(sm["rwm"].h, 7 * 8)
                for tc in range(NTC):
                    v.wait_ge(sm["dx"].h, 16 * (NCONST + tc + 1))
                    if tc >= 2:
                        v.wait_ge(sm["dout0" if tc % 2 == 0 else "dout1"].h, 16 * ((tc - 2) // 2 + 1))
                    v.tensor_reduce(stat[:, 0:1], xs[:, :], AX.X, ALU.add, negate=True)
                    v.drain()
                    v.tensor_scalar(stat[:, 0:1], stat[:, 0:1], 1.0 / DM, None, ALU.mult)
                    v.drain()
                    v.tensor_scalar(xc[:, :], xs[:, :], stat[:, 0:1], None, ALU.add)
                    v.drain()
                    v.sem_inc(sm["vL"].h, 1)
                    v.tensor_tensor(sq_t[:, :], xc[:, :], xc[:, :], ALU.mult)
                    v.drain()
                    v.tensor_reduce(stat[:, 1:2], sq_t[:, :], AX.X, ALU.add)
                    v.drain()
                    v.tensor_scalar(stat[:, 1:2], stat[:, 1:2], 1.0 / DM, None, ALU.mult)
                    v.drain()
                    v.sem_inc(sm["vL"].h, 1)
                    v.wait_ge(sm["sL"].h, L[tc]["sqrt"])
                    v.reciprocal(rstd_all[:, tc:tc + 1], stat[:, 2:3])
                    v.drain()
                    v.tensor_scalar(xn[tc % 2][:, :], xc[:, :], rstd_all[:, tc:tc + 1], None, ALU.mult)
                    v.drain()
                    v.sem_inc(sm["vL"].h, 1)
                    v.tensor_copy(xnb[:, :], xn[tc % 2][:, :])
                    v.drain()
                    v.sem_inc(sm["vL"].h, 1)
                    for f in range(NF):
                        v.wait_ge(sm["tT"].h, L[tc]["tr"][f])
                        v.tensor_copy(xnT[:, tc * DM + f * 128: tc * DM + (f + 1) * 128],
                                      ptr[f % 2][:, :]).then_inc(sm["evT"].h, 1)

            @blk.scalar
            def _(s):
                for tc in range(NTC):
                    s.wait_ge(sm["vL"].h, L[tc]["var"])
                    s.activation(stat[:, 2:3], stat[:, 1:2], AF.Sqrt, bias=eps5[:, 0:1], scale=1.0).then_inc(sm["sL"].h, 1)

            @blk.tensor
            def _(t):
                t.wait_ge(sm["dx"].h, 16 * (NCONST + 1))
                for tc in range(NTC):
                    t.wait_ge(sm["vL"].h, L[tc]["xnb"])
                    for f in range(NF):
                        n = tc * NF + f
                        if n >= 2:
                            tcp, fp = divmod(n - 2, NF)
                            t.wait_ge(sm["evT"].h, L[tcp]["trev"][fp])
                        t.transpose(ptr[f % 2][:, :], xnb[:, f * 128:(f + 1) * 128], idn[:, :]).then_inc(sm["tT"].h, 1)

            @blk.gpsimd
            def _(g):
                g.load_library(proxy_lib)
                for arm in nc.switch_core_id([g], n=128):
                    hd, pnc = int(arm) // 8, int(arm) % 8
                    if pnc not in PNC_MAP:
                        continue
                    me = PNC_MAP.index(pnc)
                    rid = HOST_RID[hd]
                    kw = 0
                    for dj in range(NCORE):
                        if dj == me:
                            continue
                        g.remote_dma(wrm[:, me * 2:(me + 1) * 2], wrs[:, :], sm["rwm"].h, sm["lwm"].h,
                                     pid=PNC_MAP[dj], routing_id=rid,
                                     dma_engine_mask=RMASK).then_inc(sm["ps1"].h, 1)
                        kw += 1
                        g.wait_ge(sm["ps1"].h, kw)
                        g.trigger_dma(1)
                    warm_preps[0] = kw
                for q in range(2):
                    dsem = sm["dw"] if q % 2 == 0 else sm["dwq"]
                    for f in range(NF):
                        g.dma_start(wq[q][:, f * 512:(f + 1) * 512],
                                    w_qkv[f * 128:(f + 1) * 128, q * 512:(q + 1) * 512]).then_inc(dsem.h, dsem.inc(16))
                for tc in range(NTC):
                    g.wait_ge(sm["vL"].h, L[tc]["xnr"])
                    dsem = sm["dout0"] if tc % 2 == 0 else sm["dout1"]
                    g.dma_start(xn_d[tc * 128:(tc + 1) * 128, :], xn[tc % 2][:, :]).then_inc(dsem.h, dsem.inc(16))

        stR1.close()
        stR2 = contextlib.ExitStack()
        wsh = stR2.enter_context(nc.sbuf_tensor("wsh", [128, DI // 2], F32, side="right"))
        rwcol = stR2.enter_context(nc.sbuf_tensor("rwcol", [128, 16], F32, side="right"))
        rmcB = stR2.enter_context(nc.sbuf_tensor("rmcB", [128, 1], F32, side="right"))

        with nc.Block() as blk:
            @blk.gpsimd
            def _(g):
                for pg in range(12):
                    if pg + 2 <= NPIECE - 1:
                        q = pg + 2
                        g.wait_ge(sm["mm"].h, piece_groups(pg)[-1]["mm"])
                        dsem = sm["dw"] if q % 2 == 0 else sm["dwq"]
                        for f in range(NF):
                            g.dma_start(wq[q % 2][:, f * 512:(f + 1) * 512],
                                        w_qkv[f * 128:(f + 1) * 128, q * 512:(q + 1) * 512]).then_inc(dsem.h, dsem.inc(16))
                    grps = piece_groups(pg)
                    if pg < 8:
                        g.wait_ge(sm["evQ"].h, grps[-1]["evQ"][3])
                        srcb = qks[pg % 2]
                    else:
                        g.wait_ge(sm["stv"].h, grps[-1]["stv"][1])
                        srcb = vst[pg % 2]
                    psem = "dstg0" if pg % 2 == 0 else "dstg1"
                    g.dma_start(stg_d[pg, :, :], srcb[:, :]).then_inc(sm[psem].h, 16)

            @blk.tensor
            def _(t):
                for n, g in enumerate(G):
                    piece, tc = g["piece"], g["tc"]
                    if tc == 0:
                        dsem = sm["dw"] if piece % 2 == 0 else sm["dwq"]
                        t.wait_ge(dsem.h, 16 * NF * (piece // 2 + 1))
                    if n >= 6:
                        t.wait_ge(sm["ev"].h, G[n - 6]["ev"])
                    bank = n % 6
                    for f in range(NF):
                        m = t.matmul(ps[bank][:, :],
                                     xnT[:, tc * DM + f * 128:tc * DM + (f + 1) * 128],
                                     wq[piece % 2][:, f * 512:(f + 1) * 512],
                                     start=(f == 0), stop=(f == NF - 1))
                        if f == NF - 1:
                            m.then_inc(sm["mm"].h, 1)
                    if g["seg"] < 2:
                        t.wait_ge(sm[g["cst_sem"]].h, g["cst"])
                        for hs in range(4):
                            nq = (n * 4 + hs)
                            if g["tr2"][hs] > 2 * 4:
                                pass
                            t.wait_ge(sm["evQ"].h, max(0, g["evQ"][hs] - 2))
                            t.transpose(ptr[hs % 2][:, :], stg[n % 2][:, hs * 128:(hs + 1) * 128],
                                        idn[:, :]).then_inc(sm["tr2"].h, 1)

            @blk.vector
            def _(v):
                for n, g in enumerate(G):
                    tc, seg = g["tc"], g["seg"]
                    b2 = n % 2
                    bank = n % 6
                    v.wait_ge(sm["mm"].h, g["mm"])
                    if n >= 2:
                        gp = G[n - 2]
                        v.wait_ge(sm[gp["cst_sem"]].h, gp["cst"])
                        if gp["seg"] < 2:
                            v.wait_ge(sm["evQ"].h, gp["evQ"][3])
                    if seg == 2:
                        pg = g["piece"]
                        if tc == 0 and pg >= 10:
                            psem = "dstg0" if pg % 2 == 0 else "dstg1"
                            v.wait_ge(sm[psem].h, 16 * ((pg - 2) // 2 + 1))
                        for dj in range(2):
                            v.tensor_copy(vst[pg % 2][:, dj * 2048 + tc * 256: dj * 2048 + (tc + 1) * 256],
                                          ps[bank][:, dj * 256:(dj + 1) * 256])
                        v.drain()
                        v.sem_inc(sm["ev"].h, 1)
                        v.sem_inc(sm["cstV"].h, 1)
                        v.sem_inc(sm["stv"].h, 2)
                    else:
                        v.tensor_copy(wk[b2][:, :], ps[bank][:, :])
                        v.drain()
                        v.sem_inc(sm["ev"].h, 1)
                        v.wait_ge(sm["ssq_s"].h, g["sq"])
                        for hs in range(4):
                            v.tensor_reduce(rn[b2][:, hs:hs + 1], ssq[b2][:, hs * 128:(hs + 1) * 128], AX.X, ALU.add)
                        v.drain()
                        v.sem_inc(sm["red"].h, 1)
                        v.wait_ge(sm["rsq"].h, g["rsq"])
                        h0 = (g["piece"] % 4) * 4
                        for hs in range(4):
                            h = h0 + hs
                            if seg == 0:
                                v.tensor_scalar(wk[b2][:, hs * 128:(hs + 1) * 128], wk[b2][:, hs * 128:(hs + 1) * 128],
                                                rn[b2][:, hs:hs + 1], gainb[:, h:h + 1], ALU.mult, ALU.mult)
                            else:
                                v.tensor_scalar(wk[b2][:, hs * 128:(hs + 1) * 128], wk[b2][:, hs * 128:(hs + 1) * 128],
                                                rn[b2][:, hs:hs + 1], None, ALU.mult)
                        v.drain()
                        co = cosb[:, tc * 64:(tc + 1) * 64]
                        si = sinb[:, tc * 64:(tc + 1) * 64]
                        for hs in (0, 2):
                            sl = wk[b2][:, hs * 128:(hs + 1) * 128]
                            t1_, t2_ = sl[:, 0:64], sl[:, 64:128]
                            A, B = rtmp[:, 0:64], rtmp[:, 64:128]
                            C, D = rtmp[:, 128:192], rtmp[:, 192:256]
                            v.tensor_tensor(A, t1_, co, ALU.mult)
                            v.tensor_tensor(B, t2_, si, ALU.mult)
                            v.tensor_tensor(C, t1_, si, ALU.mult)
                            v.tensor_tensor(D, t2_, co, ALU.mult)
                            v.drain()
                            v.tensor_tensor(t1_, A, B, ALU.subtract)
                            v.tensor_tensor(t2_, C, D, ALU.add)
                            v.drain()
                        v.sem_inc(sm["app"].h, 1)
                        pg = g["piece"]
                        if tc == 0 and pg >= 2:
                            psem = "dstg0" if pg % 2 == 0 else "dstg1"
                            v.wait_ge(sm[psem].h, 16 * ((pg - 2) // 2 + 1))
                        for hs in range(4):
                            v.wait_ge(sm["tr2"].h, g["tr2"][hs])
                            v.tensor_copy(qks[pg % 2][:, hs * 1024 + tc * 128: hs * 1024 + (tc + 1) * 128],
                                          ptr[hs % 2][:, :]).then_inc(sm["evQ"].h, 1)

            @blk.scalar
            def _(s):
                for n, g in enumerate(G):
                    if g["seg"] == 2:
                        continue
                    b2 = n % 2
                    s.wait_ge(sm["ev"].h, g["ev"])
                    s.square(ssq[b2][:, :], wk[b2][:, :]).then_inc(sm["ssq_s"].h, 1)
                    s.wait_ge(sm["red"].h, g["red"])
                    s.activation(rn[b2][:, 0:4], rn[b2][:, 0:4], AF.Ln, bias=eps6[:, 0:1], scale=1.0)
                    s.drain()
                    s.activation(rn[b2][:, 0:4], rn[b2][:, 0:4], AF.Exp, bias=0.0, scale=half_neg[:, 0:1]).then_inc(sm["rsq"].h, 1)
                    s.wait_ge(sm["app"].h, g["app"])
                    s.copy(stg[b2][:, :], wk[b2][:, :]).then_inc(sm["cstS"].h, 1)

        shard_specs = [(wg_sh, 0, RWG_N, DM), (wu_sh, RWG_N, RWU_N, DM), (wd_sh, RWG_N + RWU_N, RWD_N, DI)]
        tiles = []
        for si, (src, base, nrows, width) in enumerate(shard_specs):
            for i in range((nrows + 127) // 128):
                tiles.append((si, i, min(128, nrows - i * 128), width))

        with nc.Block() as blk:
            @blk.sync
            def _(sync):
                sync.wait_ge(sm["dout0"].h, sm["dout0"].n)
                sync.wait_ge(sm["dout1"].h, sm["dout1"].n)
                jj = 0
                for k, (si, i, rr, width) in enumerate(tiles):
                    src = shard_specs[si][0]
                    nh = (width + DI // 2 - 1) // (DI // 2)
                    for hh in range(nh):
                        w0 = hh * (DI // 2)
                        wl = min(DI // 2, width - w0)
                        if jj >= 1:
                            sync.wait_ge(sm["rx"].h, jj)
                        sync.dma_start(wsh[:rr, :wl], src[i * 128:i * 128 + rr, w0:w0 + wl]).then_inc(sm["dx"].h, sm["dx"].inc(16))
                        jj += 1

            @blk.vector
            def _(v):
                ntot = sum((shard_specs[si][3] + DI // 2 - 1) // (DI // 2) for (si, i, rr, w) in tiles)
                base_dx = sm["dx"].n - 16 * ntot
                kk2 = 0
                for k, (si, i, rr, width) in enumerate(tiles):
                    nh = (width + DI // 2 - 1) // (DI // 2)
                    for hh in range(nh):
                        w0 = hh * (DI // 2)
                        wl = min(DI // 2, width - w0)
                        kk2 += 1
                        v.wait_ge(sm["dx"].h, base_dx + 16 * kk2)
                        if hh == 0:
                            v.tensor_reduce(rwcol[:rr, k:k + 1], wsh[:rr, :wl], AX.X, ALU.max,
                                            apply_absolute_value=True)
                        else:
                            v.tensor_reduce(rmcB[:rr, 0:1], wsh[:rr, :wl], AX.X, ALU.max,
                                            apply_absolute_value=True)
                            v.drain()
                            v.tensor_tensor(rwcol[:rr, k:k + 1], rwcol[:rr, k:k + 1], rmcB[:rr, 0:1], ALU.max)
                        v.drain()
                        v.sem_inc(sm["rx"].h, 1)
                        sm["rx"].inc()

            @blk.scalar
            def _(s):
                s.wait_ge(sm["rx"].h, sm["rx"].n)
                for k, (si, i, rr, width) in enumerate(tiles):
                    base = shard_specs[si][1]
                    s.dma_start(rw_loc[base + i * 128: base + i * 128 + rr],
                                rwcol[:rr, k:k + 1]).then_inc(sm["drow"].h, sm["drow"].inc(16))
                s.wait_ge(sm["drow"].h, sm["drow"].n)
                s.dma_start(rwrow[:, :], bass.AP(rw_loc.tensor if hasattr(rw_loc, "tensor") else rw_loc,
                                                 0, [[13, 128], [1, 13]])).then_inc(sm["drow"].h, sm["drow"].inc(16))

        st1.close()
        stY = contextlib.ExitStack()
        y_g = [stY.enter_context(nc.sbuf_tensor(f"y_g{cc}", [128, 2048], BF16)) for cc in range(NCORE)]
        stBN = contextlib.ExitStack()
        bnc = [stBN.enter_context(nc.sbuf_tensor(f"bnc{i}", [128, 4096], BF16)) for i in range(2)]
        stQKV = contextlib.ExitStack()
        qk_g = [stQKV.enter_context(nc.sbuf_tensor(f"qk_g{cc}", [128, 4096], BF16)) for cc in range(NCORE)]
        v_g = [stQKV.enter_context(nc.sbuf_tensor(f"v_g{cc}", [128, 2048], BF16)) for cc in range(NCORE)]

        ps1_after_x1 = [0]
        with nc.Block() as blk:
            @blk.gpsimd
            def _(g):
                drow_total = sm["drow"].n
                for arm in nc.switch_core_id([g], n=128):
                    hd, pnc = int(arm) // 8, int(arm) % 8
                    if pnc not in PNC_MAP:
                        continue
                    me = PNC_MAP.index(pnc)
                    rid = HOST_RID[hd]
                    kk_rem = [warm_preps[0] + 1]
                    cnt = {"lsnd0": 0, "lsnd1": 0, "lself0": 0, "lself1": 0, "lrw": 0}

                    def send(dst_ap, src_ap, rsem, dj, sself, srem):
                        if dj == me:
                            cnt[sself] += 1
                            g.dma_start(dst_ap, src_ap).then_inc(sm[sself].h, 16)
                        else:
                            cnt[srem] += 1
                            g.remote_dma(dst_ap, src_ap, rsem, sm[srem].h,
                                         pid=PNC_MAP[dj], routing_id=rid,
                                         dma_engine_mask=RMASK).then_inc(sm["ps1"].h, 1)
                            g.wait_ge(sm["ps1"].h, kk_rem[0])
                            kk_rem[0] += 1
                            g.trigger_dma(1)

                    g.wait_ge(sm["dstg0"].h, 16 * (NPIECE // 2))
                    g.wait_ge(sm["dstg1"].h, 16 * (NPIECE // 2))
                    g.wait_ge(sm["drow"].h, drow_total)
                    for dj in range(NCORE):
                        send(rws_g[:, me * 13:(me + 1) * 13], rwrow[:, :], sm["rrw"].h, dj,
                             "lrw", "lrw")
                    g.wait_ge(sm["rrw"].h, 7 * 8)
                    piece_counts = []
                    for pg in range(12):
                        par = pg % 2
                        if pg >= 2:
                            pls, psf = piece_counts[pg - 2]
                            if pls:
                                g.wait_ge(sm[f"lsnd{par}"].h, 16 * pls)
                            if psf:
                                g.wait_ge(sm[f"lself{par}"].h, 16 * psf)
                        bsem = "dbn0" if par == 0 else "dbn1"
                        g.dma_start(bnc[par][:, :], stg_d[pg, :, :]).then_inc(sm[bsem].h, 16)
                        g.wait_ge(sm[bsem].h, 16 * (pg // 2 + 1))
                        if pg < 8:
                            isq = pg < 4
                            for hs in range(4):
                                h = (pg % 4) * 4 + hs
                                dj = h // 2
                                coloff = (0 if isq else 2048) + (h % 2) * 1024
                                send(qk_g[me][:, coloff:coloff + 1024],
                                     bnc[par][:, hs * 1024:(hs + 1) * 1024], sm["rqk"].h, dj,
                                     f"lself{par}", f"lsnd{par}")
                        else:
                            pr = pg - 8
                            for djl in range(2):
                                dj = 2 * pr + djl
                                for ch in range(2):
                                    send(v_g[me][:, ch * 1024:(ch + 1) * 1024],
                                         bnc[par][:, djl * 2048 + ch * 1024: djl * 2048 + (ch + 1) * 1024],
                                         sm["rv"].h, dj,
                                         f"lself{par}", f"lsnd{par}")
                        piece_counts.append((cnt[f"lsnd{par}"], cnt[f"lself{par}"]))
                    for par in range(2):
                        if cnt[f"lsnd{par}"]:
                            g.wait_ge(sm[f"lsnd{par}"].h, 16 * cnt[f"lsnd{par}"])
                        if cnt[f"lself{par}"]:
                            g.wait_ge(sm[f"lself{par}"].h, 16 * cnt[f"lself{par}"])
                    g.wait_ge(sm["lrw"].h, 16 * 8)
                    g.sem_inc(sm["xdone"].h, 1)
                    ps1_after_x1[0] = kk_rem[0] - 1


        with nc.Block() as blk:
            @blk.sync
            def _(sync):
                sync.wait_ge(sm["lrw"].h, 16 * 8)
                sync.wait_ge(sm["rrw"].h, 7 * 8)
                for cc in range(NCORE):
                    sync.dma_start(rw_full[cc * RWROW:(cc + 1) * RWROW],
                                   bass.AP(rws_g.tensor if hasattr(rws_g, "tensor") else rws_g,
                                           cc * 13, [[NCORE * 13, 128], [1, 13]])).then_inc(sm["rwa"].h, sm["rwa"].inc(16))
                sync.wait_ge(sm["rwa"].h, sm["rwa"].n)


        st2 = contextlib.ExitStack()
        e2 = st2.enter_context
        msk = e2(nc.sbuf_tensor("msk", [128, 2048], BF16))
        attn = [e2(nc.sbuf_tensor(f"attn{i}", [128, 1024], BF16)) for i in range(2)]
        acc = e2(nc.sbuf_tensor("acc", [128, 1024], BF16))
        dn512 = e2(nc.sbuf_tensor("dn512", [1, 512], F32))
        rcb = e2(nc.sbuf_tensor("rcb", [1, 512], BF16))
        ones_col = e2(nc.sbuf_tensor("ones_col", [128, 1], BF16))
        ones_row = e2(nc.sbuf_tensor("ones_row", [1, 128], BF16))
        rnorm = e2(nc.sbuf_tensor("rnorm", [128, 512], F32))
        y_stag = e2(nc.sbuf_tensor("y_stag", [128, HPC * S], BF16))
        psqk = [e2(nc.psum_tensor(f"psqk{i}", [128, 1024], F32)) for i in range(2)]
        ps_y = [e2(nc.psum_tensor(f"ps_y{i}", [128, 512], F32)) for i in range(2)]
        ps_d = e2(nc.psum_tensor("ps_d", [1, 512], F32))
        ps_b = e2(nc.psum_tensor("ps_b", [128, 512], F32))

        def q_ap(h, qb):
            cc = qb // 2
            return qk_g[cc][:, (h % 2) * 1024 + (qb % 2) * 512: (h % 2) * 1024 + (qb % 2) * 512 + 512]

        def k_ap(h, kb):
            cc = kb // 8
            return qk_g[cc][:, 2048 + (h % 2) * 1024 + (kb % 8) * 128: 2048 + (h % 2) * 1024 + (kb % 8) * 128 + 128]

        def v_ap(h, kb):
            cc = kb // 8
            return v_g[cc][:, (kb % 8) * 256 + (h % 2) * 128: (kb % 8) * 256 + (h % 2) * 128 + 128]

        with nc.Block() as blk:
            @blk.vector
            def _(v):
                v.memset(ones_col[:, :], 1.0)
                v.memset(ones_row[:, :], 1.0)
            @blk.sync
            def _(sync):
                sync.dma_start(msk[:, 0:1024], m01[:, :]).then_inc(sm["din2"].h, sm["din2"].inc(16))
                sync.dma_start(msk[:, 1024:2048], m23[:, :]).then_inc(sm["din2"].h, sm["din2"].inc(16))
                sync.wait_ge(sm["rqk"].h, 7 * 4 * 8)
                sync.wait_ge(sm["rv"].h, 7 * 2 * 8)
                sync.wait_ge(sm["xdone"].h, 1)

        with nc.Block() as blk:
            @blk.tensor
            def _(t):
                t.wait_ge(sm["din2"].h, 32)
                for nhq, d in enumerate(HQ):
                    h, qb = d["h"], d["qb"]
                    ng = len(d["grps"])
                    for gi, grp in enumerate(d["grps"]):
                        pp = psqk[gi % 2]
                        if grp["mm"] > 2:
                            t.wait_ge(sm["expd"].h, grp["mm"] - 2)
                        for j in range(2):
                            kb = gi * 2 + j
                            m = t.matmul(pp[:, j * 512:(j + 1) * 512], k_ap(h, kb), q_ap(h, qb),
                                         start=True, stop=True)
                            if j == 1:
                                m.then_inc(sm["mm2"].h, 1)
                        if gi >= ng - 2:
                            t.wait_ge(sm["accd"].h, grp["acc"])
                        else:
                            t.wait_ge(sm["expd"].h, grp["exp"])
                        if gi == 0 and nhq >= 2:
                            t.wait_ge(sm["yst"].h, nhq - 1)
                        for j in range(2):
                            kb = gi * 2 + j
                            m = t.matmul(ps_y[nhq % 2][:, :], v_ap(h, kb), attn[gi % 2][:, j * 512:(j + 1) * 512],
                                         start=(gi == 0 and j == 0), stop=(gi == ng - 1 and j == 1))
                            if j == 1:
                                m.then_inc(sm["avd"].h, 1)
                    t.wait_ge(sm["accd"].h, d["grps"][-1]["acc"])
                    if nhq >= 1:
                        t.wait_ge(sm["rcpd"].h, nhq)
                    t.matmul(ps_d[:, :], ones_col[:, :], acc[:, 0:512], start=True, stop=False)
                    t.matmul(ps_d[:, :], ones_col[:, :], acc[:, 512:1024], start=False, stop=True).then_inc(sm["dnd"].h, 1)
                    t.wait_ge(sm["rcpd"].h, d["rcp"])
                    t.matmul(ps_b[:, :], ones_row[:, :], rcb[:, :], start=True, stop=True).then_inc(sm["bcd"].h, 1)

            @blk.scalar
            def _(s):
                flat_grps = [g2 for d2 in HQ for g2 in d2["grps"]]
                gidx = 0
                for d in HQ:
                    for gi, grp in enumerate(d["grps"]):
                        s.wait_ge(sm["mm2"].h, grp["mm"])
                        if grp["mm"] > 2:
                            s.wait_ge(sm["avd"].h, grp["mm"] - 2)
                            s.wait_ge(sm["accd"].h, flat_grps[gidx - 2]["acc"])
                        s.activation(attn[gi % 2][:, :], psqk[gi % 2][:, :], AF.Exp,
                                     bias=0.0, scale=1.0).then_inc(sm["expd"].h, 1)
                        gidx += 1
                    s.wait_ge(sm["bcd"].h, d["bc"])
                    s.copy(rnorm[:, :], ps_b[:, :]).then_inc(sm["rnd"].h, 1)

            @blk.vector
            def _(v):
                for nhq, d in enumerate(HQ):
                    h, qb = d["h"], d["qb"]
                    ng = len(d["grps"])
                    v.memset(acc[:, :], 0)
                    v.drain()
                    for gi, grp in enumerate(d["grps"]):
                        v.wait_ge(sm["expd"].h, grp["exp"])
                        if gi >= ng - 2:
                            mslice = msk[:, 0:1024] if gi == ng - 2 else msk[:, 1024:2048]
                            v.tensor_tensor(attn[gi % 2][:, :], attn[gi % 2][:, :], mslice, ALU.mult)
                            v.drain()
                        v.tensor_tensor(acc[:, :], acc[:, :], attn[gi % 2][:, :], ALU.add)
                        v.drain()
                        v.sem_inc(sm["accd"].h, 1)
                    v.wait_ge(sm["dnd"].h, d["dn"])
                    v.reciprocal(dn512[:, :], ps_d[:, :])
                    v.drain()
                    v.tensor_copy(rcb[:, :], dn512[:, :])
                    v.drain()
                    v.sem_inc(sm["rcpd"].h, 1)
                    v.wait_ge(sm["avd"].h, d["grps"][-1]["av"])
                    v.wait_ge(sm["rnd"].h, d["yn"])
                    v.tensor_tensor(y_stag[:, h * S + qb * 512: h * S + (qb + 1) * 512],
                                    ps_y[nhq % 2][:, :], rnorm[:, :], ALU.mult)
                    v.drain()
                    v.sem_inc(sm["yst"].h, 1)

        with nc.Block() as blk:
            @blk.gpsimd
            def _(g):
                for arm in nc.switch_core_id([g], n=128):
                    hd, pnc = int(arm) // 8, int(arm) % 8
                    if pnc not in PNC_MAP:
                        continue
                    me = PNC_MAP.index(pnc)
                    rid = HOST_RID[hd]
                    kk_rem = [ps1_after_x1[0] + 1]
                    for h in range(HPC):
                        for qb in range(NQB):
                            dj = qb // 2
                            src = y_stag[:, h * S + qb * 512: h * S + (qb + 1) * 512]
                            dst = y_g[me][:, h * 1024 + (qb % 2) * 512: h * 1024 + (qb % 2) * 512 + 512]
                            if dj == me:
                                g.dma_start(dst, src).then_inc(sm["lself2"].h, 16)
                            else:
                                g.remote_dma(dst, src, sm["ry"].h, sm["lsnd2"].h,
                                             pid=PNC_MAP[dj], routing_id=rid,
                                             dma_engine_mask=RMASK).then_inc(sm["ps1"].h, 1)
                                g.wait_ge(sm["ps1"].h, kk_rem[0])
                                kk_rem[0] += 1
                                g.trigger_dma(1)
                g.wait_ge(sm["lsnd2"].h, 16 * 28)
                g.wait_ge(sm["lself2"].h, 16 * 4)

        st2.close()
        stQKV.close()
        stBN.close()
        stR2.close()

        st3 = contextlib.ExitStack()
        e3 = st3.enter_context
        xw = e3(nc.sbuf_tensor("xw", [128, DM], F32))
        rm_sc = e3(nc.sbuf_tensor("rm_sc", [128, 2 * NTC], F32))
        rmt_raw = e3(nc.sbuf_tensor("rmt_raw", [128, NTC], F32))
        rmt_sc = e3(nc.sbuf_tensor("rmt_sc", [128, 2 * NTC], F32))
        rmc = e3(nc.sbuf_tensor("rmc", [128, 2], F32))
        ws = [e3(nc.sbuf_tensor(f"ws{i}", [128, 1024], F32)) for i in range(2)]
        qwf = e3(nc.sbuf_tensor("qwf", [128, NP2 * 1024], FP8))
        rowp = e3(nc.sbuf_tensor("rowp", [1, 1024], F32))
        binv = e3(nc.sbuf_tensor("binv", [128, 512], F32, side="right"))
        boutg = [e3(nc.sbuf_tensor(f"boutg{i}", [128, 512], F32, side="right")) for i in range(2)]
        boutu = [e3(nc.sbuf_tensor(f"boutu{i}", [128, 512], F32, side="right")) for i in range(2)]
        tseg = [e3(nc.sbuf_tensor(f"tseg{i}", [128, 512], F32)) for i in range(2)]
        y2 = e3(nc.sbuf_tensor("y2", [128, NTC * DM], BF16))
        ps3 = [e3(nc.psum_tensor(f"ps3_{i}", [128, 512], F32)) for i in range(7)]

        def dr_lhsT(tens, width, base):
            return bass.AP(tens, base, [[width, 128], [128, 2], [1, 128]])

        def dr_rhs(slot):
            return bass.AP(qwf, slot * 1024, [[NP2 * 1024, 128], [512, 2], [1, 512]])

        stA = contextlib.ExitStack()
        qa = stA.enter_context(nc.sbuf_tensor("qa", [128, DM], FP8))
        qaT_t = stA.enter_context(nc.sbuf_tensor("qaT", [128, NTC * DM], FP8))
        gbuf = stA.enter_context(nc.sbuf_tensor("gbuf", [128, NTC * 512], F32))
        sil = stA.enter_context(nc.sbuf_tensor("sil", [128, 512], F32))
        ubuf = stA.enter_context(nc.sbuf_tensor("ubuf", [128, 512], F32))
        stPA = contextlib.ExitStack()
        pst = stPA.enter_context(nc.psum_tensor("pstA", [128, 256], FP8))
        ps8holder = {}
        def PS3B(tc):
            return ps3[tc] if tc < 7 else ps8holder["t"]

        with nc.Block() as blk:
            @blk.sync
            def _(sync):
                for tc in range(NTC):
                    if tc >= 1:
                        sync.wait_ge(sm["vA"].h, 3 * tc)
                    sync.dma_start(xw[:, :], xn_d[tc * 128:(tc + 1) * 128, :]).then_inc(sm["drp"].h, sm["drp"].inc(16))

            @blk.vector
            def _(v):
                for tc in range(NTC):
                    v.wait_ge(sm["drp"].h, 16 * (tc + 1))
                    v.tensor_reduce(rmc[:, 0:1], xw[:, :], AX.X, ALU.max, apply_absolute_value=True)
                    v.drain()
                    v.reciprocal(rmc[:, 1:2], rmc[:, 0:1])
                    v.drain()
                    v.tensor_scalar(rm_sc[:, 2 * tc:2 * tc + 1], rmc[:, 1:2], 224.0, None, ALU.mult)
                    v.tensor_scalar(rm_sc[:, 2 * tc + 1:2 * tc + 2], rmc[:, 0:1], Q2C, None, ALU.mult)
                    v.drain()
                    v.tensor_scalar(qa[:, :], xw[:, :], rm_sc[:, 2 * tc:2 * tc + 1], None, ALU.mult)
                    v.drain()
                    v.sem_inc(sm["vA"].h, 3)
                    for f in range(NDC):
                        v.wait_ge(sm["tA"].h, tc * NDC + f + 1)
                        fp, i = divmod(f, 2)
                        v.tensor_copy(qaT_t[:, tc * DM + fp * 256 + i * 128: tc * DM + fp * 256 + (i + 1) * 128],
                                      bass.AP(pst, 0, [[256, 128], [2, 128]])).then_inc(sm["evA"].h, 1)

            @blk.tensor
            def _(t):
                t.wait_ge(sm["did"].h, 16)
                for tc in range(NTC):
                    t.wait_ge(sm["vA"].h, 3 * tc + 3)
                    for f in range(NDC):
                        n = tc * NDC + f
                        if n >= 1:
                            t.wait_ge(sm["evA"].h, n)
                        t.transpose(bass.AP(pst, 0, [[256, 128], [2, 128]]),
                                    qa[:, f * 128:(f + 1) * 128], id8b[:, :]).then_inc(sm["tA"].h, 1)

        stPA.close()
        stPB = contextlib.ExitStack()
        ps8holder["t"] = stPB.enter_context(nc.psum_tensor("ps8b", [128, 512], F32))

        with nc.Block() as blk:
            @blk.sync
            def _(sync):
                for d in SCH:
                    ip = d["ip"]
                    if ip >= 1:
                        sync.wait_ge(sm["bcG"].h, ip)
                    pos = 0
                    for off, ln in rw_piece_dmas("g", ip * 512, 512):
                        sync.dma_start(bass.AP(rowp.tensor if hasattr(rowp, "tensor") else rowp, pos, [[1024, 1], [1, ln]]),
                                       rw_full[off:off + ln]).then_inc(sm["drp"].h, sm["drp"].inc(16))
                        pos += ln
                    pos = 512
                    for off, ln in rw_piece_dmas("u", ip * 512, 512):
                        sync.dma_start(bass.AP(rowp.tensor if hasattr(rowp, "tensor") else rowp, pos, [[1024, 1], [1, ln]]),
                                       rw_full[off:off + ln]).then_inc(sm["drp"].h, sm["drp"].inc(16))
                        pos += ln

            @blk.gpsimd
            def _(g):
                for d in SCH:
                    ip = d["ip"]
                    g.wait_ge(sm["drp"].h, 128 + d["rp"])
                    if ip > 1:
                        g.wait_ge(sm["tdB"].h, SCH[ip - 2]["td"][-1])
                    g.partition_broadcast(boutg[ip % 2][:, :], rowp[0:1, 0:512], channels=128)
                    g.partition_broadcast(boutu[ip % 2][:, :], rowp[0:1, 512:1024], channels=128)
                    g.drain()
                    g.sem_inc(sm["bcG"].h, 1)
                    for w, src in (("g", wgT), ("u", wuT)):
                        for fp in range(NFP):
                            buf = fp % 2
                            qm = d["qw" + w][fp]
                            if qm - 2 >= 1:
                                g.wait_ge(sm["qwB"].h, qm - 2)
                            dsn, dsv = d["dw" + w][fp]
                            g.dma_start(ws[buf][:, 0:512],
                                        src[fp * 256:fp * 256 + 128, ip * 512:(ip + 1) * 512]).then_inc(sm[dsn].h, 16)
                            g.dma_start(ws[buf][:, 512:1024],
                                        src[fp * 256 + 128:fp * 256 + 256, ip * 512:(ip + 1) * 512]).then_inc(sm[dsn].h, 16)

            @blk.vector
            def _(v):
                for d in SCH:
                    ip = d["ip"]
                    v.wait_ge(sm["bcG"].h, ip + 1)
                    for w in ("g", "u"):
                        if w == "u":
                            v.wait_ge(sm["mmB"].h, d["mmg"][-1])
                        elif ip > 0:
                            v.wait_ge(sm["mmB"].h, SCH[ip - 1]["mmu"][-1])
                        bout = boutg[ip % 2] if w == "g" else boutu[ip % 2]
                        v.reciprocal(binv[:, :], bout[:, :])
                        v.drain()
                        v.tensor_scalar(binv[:, :], binv[:, :], 224.0, None, ALU.mult)
                        v.drain()
                        for fp in range(NFP):
                            buf = fp % 2
                            dsn, dsv = d["dw" + w][fp]
                            v.wait_ge(sm[dsn].h, dsv)
                            v.tensor_tensor(qwf[:, fp * 1024:fp * 1024 + 512], ws[buf][:, 0:512], binv[:, :], ALU.mult)
                            v.tensor_tensor(qwf[:, fp * 1024 + 512:(fp + 1) * 1024], ws[buf][:, 512:1024], binv[:, :],
                                            ALU.mult).then_inc(sm["qwB"].h, 1)
                    for tc in range(NTC):
                        v.wait_ge(sm["mmB"].h, d["mmg"][tc])
                        v.scalar_tensor_tensor(gbuf[:, tc * 512:(tc + 1) * 512], PS3B(tc)[:, :],
                                               rm_sc[:, 2 * tc + 1:2 * tc + 2], boutg[ip % 2][:, :],
                                               ALU.mult, ALU.mult).then_inc(sm["evB"].h, 1)
                    for tc in range(NTC):
                        v.wait_ge(sm["mmB"].h, d["mmu"][tc])
                        v.scalar_tensor_tensor(ubuf[:, :], PS3B(tc)[:, :],
                                               rm_sc[:, 2 * tc + 1:2 * tc + 2], boutu[ip % 2][:, :],
                                               ALU.mult, ALU.mult)
                        v.drain()
                        v.wait_ge(sm["silB"].h, d["sil"][tc])
                        n0 = d["td"][tc] - 1
                        if n0 >= 2:
                            psem = "dsp0" if n0 % 2 == 0 else "dsp1"
                            v.wait_ge(sm[psem].h, 16 * ((n0 - 2) // 2 + 1))
                        v.tensor_tensor(tseg[n0 % 2][:, :], sil[:, :], ubuf[:, :], ALU.mult)
                        v.drain()
                        if ip == 0:
                            v.tensor_reduce(rmt_raw[:, tc:tc + 1], tseg[n0 % 2][:, :], AX.X, ALU.max,
                                            apply_absolute_value=True)
                        else:
                            v.tensor_reduce(rmc[:, 1:2], tseg[n0 % 2][:, :], AX.X, ALU.max,
                                            apply_absolute_value=True)
                            v.drain()
                            v.tensor_tensor(rmt_raw[:, tc:tc + 1], rmt_raw[:, tc:tc + 1], rmc[:, 1:2], ALU.max)
                        v.drain()
                        v.sem_inc(sm["tdB"].h, 1)
                v.reciprocal(rmt_sc[:, 0:NTC], rmt_raw[:, :])
                v.drain()
                v.tensor_scalar(rmt_sc[:, 0:NTC], rmt_sc[:, 0:NTC], 224.0, None, ALU.mult)
                v.tensor_scalar(rmt_sc[:, NTC:2 * NTC], rmt_raw[:, :], Q2C, None, ALU.mult)
                v.drain()
                v.sem_inc(sm["vC"].h, 1)

            @blk.scalar
            def _(s):
                nsil = 0
                for d in SCH:
                    for tc in range(NTC):
                        s.wait_ge(sm["evB"].h, d["ev"][tc])
                        if nsil >= 1:
                            s.wait_ge(sm["tdB"].h, nsil)
                        s.activation(sil[:, :], gbuf[:, tc * 512:(tc + 1) * 512], AF.Silu,
                                     bias=0.0, scale=1.0).then_inc(sm["silB"].h, 1)
                        nsil += 1
                        n0 = d["td"][tc] - 1
                        s.wait_ge(sm["tdB"].h, d["td"][tc])
                        psem = "dsp0" if n0 % 2 == 0 else "dsp1"
                        s.dma_start(t_spill[tc * 128:(tc + 1) * 128, d["ip"] * 512:(d["ip"] + 1) * 512],
                                    tseg[n0 % 2][:, :]).then_inc(sm[psem].h, 16)

            @blk.tensor
            def _(t):
                for d in SCH:
                    ip = d["ip"]
                    for w in ("g", "u"):
                        for tc in range(NTC):
                            if w == "g" and ip > 0:
                                t.wait_ge(sm["tdB"].h, SCH[ip - 1]["td"][tc])
                            if w == "u":
                                t.wait_ge(sm["evB"].h, d["ev"][tc])
                            for fp in range(NFP):
                                t.wait_ge(sm["qwB"].h, d["qw" + w][fp])
                                m = t.matmul(PS3B(tc)[:, :],
                                             dr_lhsT(qaT_t, NTC * DM, tc * DM + fp * 256),
                                             dr_rhs(fp),
                                             start=(fp == 0), stop=(fp == NFP - 1),
                                             perf_mode=DR)
                                if fp == NFP - 1:
                                    m.then_inc(sm["mmB"].h, 1)

        stA.close()

        stPB.close()
        stPC = contextlib.ExitStack()
        pst2 = stPC.enter_context(nc.psum_tensor("pstC", [128, 256], FP8))
        stC = contextlib.ExitStack()
        tw = stC.enter_context(nc.sbuf_tensor("tw", [128, DI], F32))
        qt_t = stC.enter_context(nc.sbuf_tensor("qt", [128, DI], FP8))
        qtT_t = stC.enter_context(nc.sbuf_tensor("qtT", [128, NTC * DI], FP8))

        with nc.Block() as blk:
            @blk.sync
            def _(sync):
                nspill = NIP * NTC
                sync.wait_ge(sm["dsp0"].h, 16 * ((nspill + 1) // 2))
                sync.wait_ge(sm["dsp1"].h, 16 * (nspill // 2))
                for tc in range(NTC):
                    if tc >= 1:
                        sync.wait_ge(sm["vC"].h, 1 + tc)
                    sync.dma_start(tw[:, :], t_spill[tc * 128:(tc + 1) * 128, :]).then_inc(sm["dtC"].h, 16)

            @blk.vector
            def _(v):
                for tc in range(NTC):
                    v.wait_ge(sm["dtC"].h, 16 * (tc + 1))
                    v.tensor_scalar(qt_t[:, :], tw[:, :], rmt_sc[:, tc:tc + 1], None, ALU.mult)
                    v.drain()
                    v.sem_inc(sm["vC"].h, 1)
                    for ic in range(DI // 128):
                        v.wait_ge(sm["tC"].h, tc * (DI // 128) + ic + 1)
                        ip2, i = divmod(ic, 2)
                        v.tensor_copy(qtT_t[:, tc * DI + ip2 * 256 + i * 128: tc * DI + ip2 * 256 + (i + 1) * 128],
                                      bass.AP(pst2, 0, [[256, 128], [2, 128]])).then_inc(sm["evC"].h, 1)

            @blk.tensor
            def _(t):
                for tc in range(NTC):
                    t.wait_ge(sm["vC"].h, 2 + tc)
                    for ic in range(DI // 128):
                        n = tc * (DI // 128) + ic
                        if n >= 1:
                            t.wait_ge(sm["evC"].h, n)
                        t.transpose(bass.AP(pst2, 0, [[256, 128], [2, 128]]),
                                    qt_t[:, ic * 128:(ic + 1) * 128], id8b[:, :]).then_inc(sm["tC"].h, 1)

        stPC.close()
        stPD = contextlib.ExitStack()
        ps8holder["t"] = stPD.enter_context(nc.psum_tensor("ps8d", [128, 512], F32))

        DRP_BASE_D = [sm["drp"].n]
        with nc.Block() as blk:
            @blk.sync
            def _(sync):
                for d in SD:
                    hp = d["hp"]
                    if hp >= 1:
                        sync.wait_ge(sm["bcD"].h, hp)
                    pos = 0
                    for off, ln in rw_piece_dmas("d", hp * 512, 512):
                        sync.dma_start(bass.AP(rowp, pos, [[1024, 1], [1, ln]]),
                                       rw_full[off:off + ln]).then_inc(sm["drp"].h, sm["drp"].inc(16))
                        pos += ln

            @blk.gpsimd
            def _(g):
                base = DRP_BASE_D[0]
                for d in SD:
                    hp = d["hp"]
                    g.wait_ge(sm["drp"].h, base + d["rp"])
                    if hp > 1:
                        g.wait_ge(sm["evD"].h, SD[hp - 2]["ev"][-1])
                    g.partition_broadcast(boutg[hp % 2][:, :], rowp[0:1, 0:512], channels=128)
                    g.drain()
                    g.sem_inc(sm["bcD"].h, 1)
                    for ip2 in range(NP2):
                        buf = ip2 % 2
                        qm = d["qw"][ip2]
                        if qm - 2 >= 1:
                            g.wait_ge(sm["qwD"].h, qm - 2)
                        dsn, dsv = d["dw"][ip2]
                        g.dma_start(ws[buf][:, 0:512],
                                    wd2T[ip2 * 256:ip2 * 256 + 128, hp * 512:(hp + 1) * 512]).then_inc(sm[dsn].h, 16)
                        g.dma_start(ws[buf][:, 512:1024],
                                    wd2T[ip2 * 256 + 128:ip2 * 256 + 256, hp * 512:(hp + 1) * 512]).then_inc(sm[dsn].h, 16)

            @blk.vector
            def _(v):
                for d in SD:
                    hp = d["hp"]
                    v.wait_ge(sm["bcD"].h, hp + 1)
                    if hp > 0:
                        v.wait_ge(sm["mmD"].h, SD[hp - 1]["mm"][-1])
                    v.reciprocal(binv[:, :], boutg[hp % 2][:, :])
                    v.drain()
                    v.tensor_scalar(binv[:, :], binv[:, :], 224.0, None, ALU.mult)
                    v.drain()
                    for ip2 in range(NP2):
                        buf = ip2 % 2
                        dsn, dsv = d["dw"][ip2]
                        v.wait_ge(sm[dsn].h, dsv)
                        v.tensor_tensor(qwf[:, ip2 * 1024:ip2 * 1024 + 512], ws[buf][:, 0:512], binv[:, :], ALU.mult)
                        v.tensor_tensor(qwf[:, ip2 * 1024 + 512:(ip2 + 1) * 1024], ws[buf][:, 512:1024], binv[:, :],
                                        ALU.mult).then_inc(sm["qwD"].h, 1)
                    for tc in range(NTC):
                        v.wait_ge(sm["mmD"].h, d["mm"][tc])
                        v.scalar_tensor_tensor(y2[:, tc * DM + hp * 512: tc * DM + (hp + 1) * 512], PS3B(tc)[:, :],
                                               rmt_sc[:, NTC + tc:NTC + tc + 1], boutg[hp % 2][:, :],
                                               ALU.mult, ALU.mult).then_inc(sm["evD"].h, 1)

            @blk.tensor
            def _(t):
                for d in SD:
                    hp = d["hp"]
                    for tc in range(NTC):
                        if hp > 0:
                            t.wait_ge(sm["evD"].h, SD[hp - 1]["ev"][tc])
                        for ip2 in range(NP2):
                            t.wait_ge(sm["qwD"].h, d["qw"][ip2])
                            m = t.matmul(PS3B(tc)[:, :],
                                         dr_lhsT(qtT_t, NTC * DI, tc * DI + ip2 * 256),
                                         dr_rhs(ip2),
                                         start=(ip2 == 0), stop=(ip2 == NP2 - 1),
                                         perf_mode=DR)
                            if ip2 == NP2 - 1:
                                m.then_inc(sm["mmD"].h, 1)

        stC.close()

        stE = contextlib.ExitStack()
        wob = [stE.enter_context(nc.sbuf_tensor(f"wob{i}", [128, 512], BF16)) for i in range(2)]

        with nc.Block() as blk:
            @blk.gpsimd
            def _(g):
                g.wait_ge(sm["ry"].h, 7 * 4 * 8)
                g.wait_ge(sm["lself2"].h, 4 * 16)
                for op in range(NHP):
                    d = SE[op]
                    for dc in range(NDC):
                        n = op * NDC + dc
                        if n >= 2:
                            pn = n - 2
                            pd = SE[pn // NDC]
                            if pd["wu"][pn % NDC] is not None:
                                g.wait_ge(sm["bufE"].h, pd["wu"][pn % NDC])
                            else:
                                g.wait_ge(sm["evE"].h, pd["ev"][-1])
                        dsn, dsv = d["dw"][dc]
                        g.dma_start(wob[dc % 2][:, :],
                                    w_out[dc * 128:(dc + 1) * 128, op * 512:(op + 1) * 512]).then_inc(sm[dsn].h, 16)

            @blk.tensor
            def _(t):
                for op in range(NHP):
                    d = SE[op]
                    for dc in range(NDC):
                        dsn, dsv = d["dw"][dc]
                        t.wait_ge(sm[dsn].h, dsv)
                        for tc in range(NTC):
                            if dc == 0 and op > 0:
                                t.wait_ge(sm["evE"].h, SE[op - 1]["ev"][tc])
                            m = t.matmul(PS3B(tc)[:, :],
                                         y_g[dc // 2][:, (dc % 2) * 1024 + tc * 128: (dc % 2) * 1024 + (tc + 1) * 128],
                                         wob[dc % 2][:, :],
                                         start=(dc == 0), stop=(dc == NDC - 1))
                            if tc == NTC - 1 and dc < NDC - 1:
                                m.then_inc(sm["bufE"].h, 1)
                            if dc == NDC - 1:
                                m.then_inc(sm["mmE"].h, 1)

            @blk.vector
            def _(v):
                for op in range(NHP):
                    d = SE[op]
                    for tc in range(NTC):
                        v.wait_ge(sm["mmE"].h, d["mm"][tc])
                        v.tensor_tensor(y2[:, tc * DM + op * 512: tc * DM + (op + 1) * 512],
                                        y2[:, tc * DM + op * 512: tc * DM + (op + 1) * 512],
                                        PS3B(tc)[:, :], ALU.add)
                        v.drain()
                        v.sem_inc(sm["evE"].h, 1)

        stE.close()
        stPD.close()
        stF = contextlib.ExitStack()
        ostg = [stF.enter_context(nc.sbuf_tensor(f"ostg{i}", [128, DM], F32)) for i in range(2)]

        with nc.Block() as blk:
            @blk.sync
            def _(sync):
                for tc in range(NTC):
                    if tc >= 1:
                        sync.wait_ge(sm["vF"].h, tc)
                    sync.dma_start(xw[:, :], x[tc * 128:(tc + 1) * 128, :]).then_inc(sm["dxF"].h, 16)

            @blk.vector
            def _(v):
                for tc in range(NTC):
                    v.wait_ge(sm["dxF"].h, 16 * (tc + 1))
                    if tc >= 2:
                        v.wait_ge(sm["doF0" if tc % 2 == 0 else "doF1"].h, 16 * ((tc - 2) // 2 + 1))
                    v.tensor_tensor(ostg[tc % 2][:, :], xw[:, :], y2[:, tc * DM:(tc + 1) * DM], ALU.add)
                    v.drain()
                    v.sem_inc(sm["vF"].h, 1)

            @blk.gpsimd
            def _(g):
                for tc in range(NTC):
                    g.wait_ge(sm["vF"].h, tc + 1)
                    dsem = sm["doF0"] if tc % 2 == 0 else sm["doF1"]
                    g.dma_start(out_o[tc * 128:(tc + 1) * 128, :], ostg[tc % 2][:, :]).then_inc(dsem.h, dsem.inc(16))
                g.wait_ge(sm["doF0"].h, sm["doF0"].n)
                g.wait_ge(sm["doF1"].h, sm["doF1"].n)
        stF.close()
        st3.close()
        stY.close()
    return nc


_NC_CACHE = {}

def _get_nc(name, builder):
    if name not in _NC_CACHE:
        nc = builder()
        nc.compile()
        _NC_CACHE[name] = nc
    return _NC_CACHE[name]


def kernel(L0_attn_x, L0_attn_w_qkv, L0_attn_w_out, L0_attn_q_gain,
           L0_ffn_w_gate, L0_ffn_w_up, L0_ffn_w_down,
           L0_ffn_up_reorder_index, L0_ffn_down_reorder_index):
    import ml_dtypes
    from concourse.bass_utils import run_bass_kernel_spmd

    x_full = np.ascontiguousarray(np.asarray(L0_attn_x)[0])
    w_qkv = np.ascontiguousarray(np.asarray(L0_attn_w_qkv))
    w_out = np.ascontiguousarray(np.asarray(L0_attn_w_out))
    q_gain = np.asarray(L0_attn_q_gain)
    wg = np.ascontiguousarray(np.asarray(L0_ffn_w_gate))
    wu = np.ascontiguousarray(np.asarray(L0_ffn_w_up))
    wd = np.ascontiguousarray(np.asarray(L0_ffn_w_down))
    di_idx = np.asarray(L0_ffn_down_reorder_index)
    wd2 = np.ascontiguousarray(wd[:, np.argsort(di_idx)])

    half = 64
    inv_freq = (10000.0 ** (-(np.arange(half, dtype=np.float32)) / half)).astype(np.float32)
    theta = (np.arange(S, dtype=np.float32)[:, None] * inv_freq[None, :]).astype(np.float32)
    cos_full = np.cos(theta).astype(np.float32)
    sin_full = np.sin(theta).astype(np.float32)
    ident = np.eye(128, dtype=np.float32).astype(ml_dtypes.bfloat16)
    id8 = np.eye(128, dtype=np.float32).astype(ml_dtypes.float8_e4m3)
    gain_b = np.ascontiguousarray(np.broadcast_to(q_gain[None, :].astype(np.float32), (128, NH)))
    jj = np.arange(512)[None, :]
    tt = np.arange(128)[:, None]
    masks = [(jj >= j * 128 + tt).astype(np.float32) for j in range(4)]
    m01 = np.ascontiguousarray(np.concatenate(masks[0:2], 1)).astype(ml_dtypes.bfloat16)
    m23 = np.ascontiguousarray(np.concatenate(masks[2:4], 1)).astype(ml_dtypes.bfloat16)
    wgT_np = np.ascontiguousarray(wg.T)
    wuT_np = np.ascontiguousarray(wu.T)
    wd2T_np = np.ascontiguousarray(wd2.T)

    nc = _get_nc("fused", build_fused)
    ins = []
    for c in range(NCORE):
        ins.append({
            "x": np.ascontiguousarray(x_full[c * TPC:(c + 1) * TPC]),
            "w_qkv": w_qkv,
            "cos_t": np.ascontiguousarray(cos_full[c * TPC:(c + 1) * TPC]),
            "sin_t": np.ascontiguousarray(sin_full[c * TPC:(c + 1) * TPC]),
            "gain_b": gain_b,
            "ident": ident,
            "id8": id8,
            "wg_sh": np.ascontiguousarray(wg[c * RWG_N:(c + 1) * RWG_N]),
            "wu_sh": np.ascontiguousarray(wu[c * RWU_N:(c + 1) * RWU_N]),
            "wd_sh": np.ascontiguousarray(wd2[c * RWD_N:(c + 1) * RWD_N]),
            "m01": m01, "m23": m23,
            "w_out": w_out,
            "wgT": wgT_np, "wuT": wuT_np, "wd2T": wd2T_np,
        })
    r = run_bass_kernel_spmd(nc, ins, core_ids=list(range(NCORE)))
    out = np.concatenate([np.asarray(r.results[c]["out_o"]) for c in range(NCORE)], axis=0)
    return out[None, :, :].astype(np.float32)
